# version 20
# baseline (speedup 1.0000x reference)
import numpy as np
import ml_dtypes
from contextlib import ExitStack

import concourse.bass as bass
import concourse.mybir as mybir
from concourse.bass_utils import run_bass_kernel_spmd

F32 = mybir.dt.float32
BF16 = mybir.dt.bfloat16
U8 = mybir.dt.uint8
AF = mybir.ActivationFunctionType
OP = mybir.AluOpType
AX = mybir.AxisListType.X

D = 2048
NDC = 16
NH = 16
HD = 128
TQ = 1024
TKV = 2048
NQT = 8
NKT = 16
II = 8192
NIC = 64
TT = 512
NTT = 2
QK_SCALE = 1.0 / float(np.sqrt(HD))


class Sem:
    def __init__(self, name):
        self.name = name
        self.h = None
        self.n = 0


def build_nc(taps=()):
    nc = bass.Bass("TRN2", target_bir_lowering=False)

    xkv_d = nc.dram_tensor("xkv", [D, TKV], F32, kind="ExternalInput")
    xq_d = nc.dram_tensor("xq", [D, TQ], F32, kind="ExternalInput")
    wqkv_d = nc.dram_tensor("wqkv", [48, 128, NDC, 128], F32, kind="ExternalInput")
    wout_d = nc.dram_tensor("wout", [16, 128, NDC, 128], F32, kind="ExternalInput")
    wgu_d = nc.dram_tensor("wgu", [NIC, 128, 4096], U8, kind="ExternalInput")
    wd_d = nc.dram_tensor("wd", [16, 128, NIC, 128], U8, kind="ExternalInput")
    bqkv_d = nc.dram_tensor("bqkv", [128, 48], F32, kind="ExternalInput")
    bout_d = nc.dram_tensor("bout", [128, 16], F32, kind="ExternalInput")
    gsc_d = nc.dram_tensor("gsc", [128, 64], F32, kind="ExternalInput")
    usc_d = nc.dram_tensor("usc", [128, 64], F32, kind="ExternalInput")
    dsc_d = nc.dram_tensor("dsc", [128, 16], F32, kind="ExternalInput")
    mask_d = nc.dram_tensor("mask", [128, 256], F32, kind="ExternalInput")
    ident_d = nc.dram_tensor("identw", [128, 128], BF16, kind="ExternalInput")
    yt_d = nc.dram_tensor("yt", [D, TQ], F32, kind="ExternalOutput")
    tap_shapes = {
        "qt": [128, 16384], "kt15": [128, 2048], "v15": [128, 2048],
        "oattn": [128, 16384], "resA": [128, 16384], "xn1": [128, 16384],
        "h0": [128, 32768], "p15_7": [128, 2048], "ydown": [128, 16384],
    }
    tap_d = {t: nc.dram_tensor("tap_" + t, [128, tap_shapes[t][1]], F32,
                               kind="ExternalOutput") for t in taps}

    plan = []

    def E(eng, f):
        plan.append((eng, f))

    sems = [Sem(s) for s in
            ("dC", "dW", "dG", "dO", "dB", "eT", "eS", "eV", "eG")]
    dC, dW, dG, dO, dB, eT, eS, eV, eG = sems

    def W(eng, sem, val):
        assert val <= sem.n, (eng, sem.name, val, sem.n)
        if val <= 0:
            return
        E(eng, lambda e, s=sem, v=val: e.wait_ge(s.h, v))

    def I(eng, sem, f, k=1):
        sem.n += k
        E(eng, lambda e, f=f, s=sem, k=k: f(e).then_inc(s.h, k))
        return sem.n

    def N(eng, f):
        E(eng, lambda e, f=f: f(e))

    with ExitStack() as es:
        ent = es.enter_context
        AXa = ent(nc.sbuf_tensor("AXa", [128, 65536], U8))
        AZa = ent(nc.sbuf_tensor("AZa", [128, 65536], U8))
        AWa = ent(nc.sbuf_tensor("AWa", [128, 32768], U8))
        wbf = [ent(nc.sbuf_tensor(f"wbf{i}", [128, 2048], BF16)) for i in range(4)]
        stg = [ent(nc.sbuf_tensor(f"stg{i}", [128, 2048], U8)) for i in range(2)]
        stg4 = [ent(nc.sbuf_tensor(f"stg4{i}", [128, 4096], U8)) for i in range(2)]
        bqkv_s = ent(nc.sbuf_tensor("bqkv_s", [128, 48], F32))
        bqs = ent(nc.sbuf_tensor("bqs", [128, 16], F32))
        bout_s = ent(nc.sbuf_tensor("bout_s", [128, 16], F32))
        gsc_s = ent(nc.sbuf_tensor("gsc_s", [128, 64], F32))
        usc_s = ent(nc.sbuf_tensor("usc_s", [128, 64], F32))
        dsc_s = ent(nc.sbuf_tensor("dsc_s", [128, 16], F32))
        mask_s = ent(nc.sbuf_tensor("mask_s", [128, 256], F32))
        ident = ent(nc.sbuf_tensor("ident", [128, 128], BF16))
        onescol = ent(nc.sbuf_tensor("onescol", [128, 1], BF16))
        c128 = ent(nc.sbuf_tensor("c128", [128, 1], F32))
        c_eps = ent(nc.sbuf_tensor("c_eps", [1, 1], F32))
        negm = ent(nc.sbuf_tensor("negm", [128, 1], F32))
        sumc = ent(nc.sbuf_tensor("sumc", [128, 4], F32))
        sums = ent(nc.sbuf_tensor("sums", [128, 1], F32))
        rvec = ent(nc.sbuf_tensor("rvec", [128, 1], F32))
        scr4 = ent(nc.sbuf_tensor("scr4", [128, 4], F32))
        mx4 = ent(nc.sbuf_tensor("mx4", [128, 4], F32))
        gs_buf = [ent(nc.sbuf_tensor(f"gsb{i}", [128, 512], F32)) for i in range(2)]
        m_row = ent(nc.sbuf_tensor("m_row", [1, 512], F32))
        msq_row = ent(nc.sbuf_tensor("msq_row", [1, 512], F32))
        m2_row = ent(nc.sbuf_tensor("m2_row", [1, 512], F32))
        r_row = ent(nc.sbuf_tensor("r_row", [1, 512], F32))
        mbc = ent(nc.sbuf_tensor("mbc", [128, 512], F32))
        rbc = ent(nc.sbuf_tensor("rbc", [128, 512], F32))
        tmpb = [gs_buf[0][:].bitcast(BF16)[:, 0:512], gs_buf[1][:].bitcast(BF16)[:, 0:512]]
        tmps = [mbc[:].bitcast(BF16)[:, 0:512], rbc[:].bitcast(BF16)[:, 0:512]]
        SB = ent(nc.psum_tensor("SB", [128, 2048], F32))
        PTa = ent(nc.psum_tensor("PTa", [128, 1024], BF16))
        PTb = ent(nc.psum_tensor("PTb", [128, 1024], BF16))
        OPp = ent(nc.psum_tensor("OPp", [128, 1024], F32))
        for s in sems:
            s.h = ent(nc.semaphore(s.name))
        block = ent(nc.Block())

        def reg(arena, off_bytes, n_elems, dt):
            esz = {F32: 4, BF16: 2, U8: 1}[dt]
            return arena[:, off_bytes: off_bytes + n_elems * esz].bitcast(dt)

        Xbkv = reg(AXa, 0, 32768, BF16)
        XownA = reg(AXa, 0, 16384, F32)
        Hbuf = reg(AXa, 0, 32768, BF16)
        QT = reg(AZa, 0, 16384, BF16)
        Oattn = reg(AZa, 32768, 16384, BF16)
        XN1 = reg(AZa, 0, 16384, F32)
        KT2 = [reg(AWa, 0, 2048, BF16), reg(AWa, 4096, 2048, BF16)]
        Vb2 = [reg(AWa, 8192, 2048, BF16), reg(AWa, 12288, 2048, BF16)]
        Ssb = reg(AWa, 16384, 2048, F32)
        Pbs = reg(AWa, 24576, 2048, BF16)
        Pt = reg(AWa, 28672, 2048, BF16)
        XN1b = reg(AWa, 0, 16384, BF16)

        buf_tok = {}

        def need(eng, key):
            if key in buf_tok:
                s, v = buf_tok[key]
                W(eng, s, v)

        def used(key, sem):
            buf_tok[key] = (sem, sem.n)

        def tap(name, src_ap, wsem, wtok):
            if name not in tap_d:
                return None
            W("gpsimd", wsem, wtok)
            I("gpsimd", dO, lambda e, name=name, src_ap=src_ap: e.dma_start(
                tap_d[name].ap(), src_ap), 16)
            return dO.n

        for dst, src in ((bqkv_s, bqkv_d), (bout_s, bout_d), (gsc_s, gsc_d),
                         (usc_s, usc_d), (dsc_s, dsc_d), (mask_s, mask_d),
                         (ident, ident_d)):
            I("sync", dC, lambda e, d=dst, s=src: e.dma_start(d[:], s.ap()), 16)
        tok_dC = dC.n

        N("vector", lambda e: e.memset(onescol[:], 1.0))
        N("vector", lambda e: e.memset(c128[:], -128.0))
        N("vector", lambda e: e.memset(c_eps[:], 1e-5))
        W("vector", dC, tok_dC)
        tok_memset = I("vector", eV, lambda e: e.tensor_scalar_mul(
            bqs[:], bqkv_s[:, 0:16], QK_SCALE))

        xkv_tok = []
        for dc in range(NDC):
            I("gpsimd", dG, lambda e, dc=dc: e.dma_start(
                Xbkv[:, dc * 2048:(dc + 1) * 2048],
                xkv_d.ap()[dc * 128:(dc + 1) * 128, :]), 16)
            xkv_tok.append(dG.n)
        I("gpsimd", eG, lambda e: e.drain())
        xcast_done = eG.n

        W("tensor", dC, tok_dC)
        W("scalar", dC, tok_dC)
        W("scalar", eV, tok_memset)
        q_evac_done = 0
        tap("qt", QT[:, :], eS, q_evac_done)
        last_xbkv_mm = [None]

        VtS = [stg[0][:].bitcast(BF16), stg[1][:].bitcast(BF16)]
        o_evac_tok = {}
        sb_read_tok = [0, 0]
        unit_evac = [q_evac_done, q_evac_done]
        pscale_prev = 0
        trj_prev = 0
        avj_prev = 0
        ptevac_prev = 0
        prev_exp = 0
        kvinfo = {}
        qt_done = {}
        kt_ready = {}
        vt_ready = {}

        def kv_dma(hh):
            qb, kb, vb = (3 * hh) % 4, (3 * hh + 1) % 4, (3 * hh + 2) % 4
            need("gpsimd", f"wbf{qb}")
            W("gpsimd", dG, dG.n)
            I("gpsimd", dG, lambda e, hh=hh, qb=qb: e.dma_start(
                wbf[qb][:], wqkv_d.ap()[hh].rearrange("p c f -> p (c f)")), 16)
            qd = dG.n
            I("gpsimd", eG, lambda e: e.drain())
            qt_ = eG.n
            need("gpsimd", f"wbf{kb}")
            W("gpsimd", dG, dG.n)
            I("gpsimd", dG, lambda e, hh=hh, kb=kb: e.dma_start(
                wbf[kb][:], wqkv_d.ap()[16 + hh].rearrange("p c f -> p (c f)")), 16)
            kd = dG.n
            I("gpsimd", eG, lambda e: e.drain())
            kt_ = eG.n
            need("gpsimd", f"wbf{vb}")
            W("gpsimd", dG, dG.n)
            I("gpsimd", dG, lambda e, hh=hh, vb=vb: e.dma_start(
                wbf[vb][:], wqkv_d.ap()[32 + hh].rearrange("p c f -> p (c f)")), 16)
            vd = dG.n
            I("gpsimd", eG, lambda e: e.drain())
            vt_ = eG.n
            kvinfo[hh] = (qb, kb, vb, qd, qt_, kd, kt_, vd, vt_)
            kt_ready[hh] = [None] * 4
            qt_done[hh] = [None, None]

        def proj_unit_mm(hh, unit):
            qb, kb, vb, qd, qt_, kd, kt_, vd, vt_ = kvinfo[hh]
            b = 2 + (unit % 2)
            kind = "v" if unit < 4 else ("k" if unit < 8 else "q")
            tc = unit if unit < 4 else (unit - 4 if unit < 8 else unit - 8)
            wb_i = {"v": vb, "k": kb, "q": qb}[kind]
            dtok, gtok = {"v": (vd, vt_), "k": (kd, kt_), "q": (qd, qt_)}[kind]
            W("tensor", dG, dtok)
            W("tensor", eG, gtok)
            W("tensor", eS, unit_evac[unit % 2])
            for dc in range(NDC):
                if kind == "q":
                    rhs = Xbkv[:, dc * 2048:(dc + 1) * 2048].rearrange(
                        "p (a b2 f) -> p a b2 f", a=8, b2=2)[:, tc * 4:(tc + 1) * 4, 0:1, :]
                else:
                    rhs = Xbkv[:, dc * 2048 + tc * 512: dc * 2048 + tc * 512 + 512]
                f = (lambda e, wb_i=wb_i, dc=dc, b=b, rhs=rhs:
                     e.matmul(SB[:, b * 512:(b + 1) * 512],
                              wbf[wb_i][:, dc * 128:(dc + 1) * 128],
                              rhs, start=(dc == 0), stop=(dc == NDC - 1)))
                if dc == NDC - 1:
                    I("tensor", eT, f)
                else:
                    N("tensor", f)
            last_xbkv_mm[0] = eT.n
            if unit == 3:
                used(f"wbf{vb}", eT)
            if unit == 7:
                used(f"wbf{kb}", eT)
            if unit >= 8:
                used(f"wbf{qb}", eT)
            return eT.n, b, kind, tc

        def proj_unit_evac(hh, unit, mm_tok, b, kind, tc):
            W("scalar", eT, mm_tok)
            if kind == "v":
                need("scalar", f"stg{tc // 2}")
                I("scalar", eS, lambda e, hh=hh, b=b, tc=tc: e.activation(
                    VtS[tc // 2][:, (tc % 2) * 512:(tc % 2) * 512 + 512],
                    SB[:, b * 512:(b + 1) * 512],
                    AF.Identity, bias=bqkv_s[:, 32 + hh:33 + hh]))
                vt_ready[hh] = eS.n
            elif kind == "k":
                need("scalar", f"kt{hh % 2}")
                I("scalar", eS, lambda e, hh=hh, b=b, tc=tc: e.activation(
                    KT2[hh % 2][:, tc * 512:(tc + 1) * 512],
                    SB[:, b * 512:(b + 1) * 512],
                    AF.Identity, bias=bqkv_s[:, 16 + hh:17 + hh]))
                kt_ready[hh][tc] = eS.n
            else:
                I("scalar", eS, lambda e, hh=hh, b=b, tc=tc: e.activation(
                    QT[:, hh * 1024 + tc * 512: hh * 1024 + tc * 512 + 512],
                    SB[:, b * 512:(b + 1) * 512],
                    AF.Identity, bias=bqs[:, hh:hh + 1], scale=QK_SCALE))
                qt_done[hh][tc] = eS.n
            unit_evac[unit % 2] = eS.n

        def v_transposes(hh):
            W("tensor", eS, vt_ready[hh])
            if ptevac_prev:
                W("tensor", eV, ptevac_prev)
            for m in range(NKT):
                tgt = (PTa if m < 8 else PTb)
                f = (lambda e, m=m, tgt=tgt:
                     e.transpose(tgt[:, (m % 8) * 128:(m % 8) * 128 + 128],
                                 VtS[m // 8][:, (m % 8) * 128:(m % 8) * 128 + 128],
                                 ident[:]))
                if m in (7, 15):
                    I("tensor", eT, f)
                else:
                    N("tensor", f)
            vtr = eT.n
            used("stg0", eT)
            used("stg1", eT)
            need("vector", f"vb{hh % 2}")
            W("vector", eT, vtr)
            I("vector", eV, lambda e, hh=hh: e.tensor_copy(Vb2[hh % 2][:, 0:1024], PTa[:]))
            I("vector", eV, lambda e, hh=hh: e.tensor_copy(Vb2[hh % 2][:, 1024:2048], PTb[:]))
            return eV.n

        kv_dma(0)
        for u in range(10):
            mtok, b, kind, tc = proj_unit_mm(0, u)
            proj_unit_evac(0, u, mtok, b, kind, tc)

        for h in range(NH):
            if h + 1 < NH:
                kv_dma(h + 1)
            ptevac_cur = v_transposes(h)
            ptevac_prev = ptevac_cur
            for j in range(NQT):
                L = 256 * (j + 1)
                ncj = (L + 511) // 512
                for kc in range(ncj):
                    nn = min(512, L - kc * 512)
                    bank = kc % 2
                    if kc == 0:
                        W("tensor", eS, qt_done[h][j // 4])
                    if sb_read_tok[bank]:
                        W("tensor", eV, sb_read_tok[bank])
                    W("tensor", eS, kt_ready[h][kc])
                    I("tensor", eT, lambda e, h=h, j=j, kc=kc, nn=nn, bank=bank:
                      e.matmul(SB[:, bank * 512: bank * 512 + nn],
                               QT[:, h * 1024 + j * 128: h * 1024 + j * 128 + 128],
                               KT2[h % 2][:, kc * 512: kc * 512 + nn],
                               start=True, stop=True))
                    mm_tok = eT.n
                    W("vector", eT, mm_tok)
                    if kc == 0 and pscale_prev:
                        W("vector", eS, pscale_prev)
                    I("vector", eV, lambda e, kc=kc, nn=nn, bank=bank: e.tensor_scalar(
                        Ssb[:, kc * 512: kc * 512 + nn], SB[:, bank * 512: bank * 512 + nn],
                        0.0, -3.0e38, OP.add, OP.max, accum_out=mx4[:, kc:kc + 1]))
                    sb_read_tok[bank] = eV.n
                if h + 1 < NH:
                    u_mm = proj_unit_mm(h + 1, j)
                    u_mm2 = proj_unit_mm(h + 1, 15 - j) if j >= 6 else None
                if prev_exp:
                    W("vector", eS, prev_exp)
                N("vector", lambda e: e.drain())
                N("vector", lambda e, ncj=ncj: e.reduce_max(
                    negm[:], mx4[:, 0:ncj], axis=AX, negate=True))
                N("vector", lambda e: e.drain())
                N("vector", lambda e, L=L: e.tensor_tensor(
                    Ssb[:, L - 256:L], Ssb[:, L - 256:L], mask_s[:], OP.add))
                I("vector", eV, lambda e: e.drain())
                mask_tok = eV.n
                W("scalar", eV, mask_tok)
                for kc in range(ncj):
                    nn = min(512, L - kc * 512)
                    I("scalar", eS, lambda e, kc=kc, nn=nn: e.activation(
                        Ssb[:, kc * 512: kc * 512 + nn], Ssb[:, kc * 512: kc * 512 + nn],
                        AF.Exp, bias=negm[:], accum_out=sumc[:, kc:kc + 1]))
                N("scalar", lambda e: e.drain())
                N("scalar", lambda e, ncj=ncj: e.activation(
                    scr4[:, 0:ncj], sumc[:, 0:ncj], AF.Identity, accum_out=sums[:]))
                I("scalar", eS, lambda e: e.drain())
                expj_tok = eS.n
                prev_exp = expj_tok
                W("vector", eS, expj_tok)
                N("vector", lambda e: e.reciprocal(rvec[:], sums[:]))
                I("vector", eV, lambda e: e.drain())
                rv_tok = eV.n
                W("scalar", eV, rv_tok)
                if trj_prev:
                    W("scalar", eT, trj_prev)
                I("scalar", eS, lambda e, L=L: e.activation(
                    Pbs[:, 0:L], Ssb[:, 0:L], AF.Identity, scale=rvec[:]))
                pscale_tok = eS.n
                pscale_prev = pscale_tok
                W("tensor", eS, pscale_tok)
                W("tensor", eV, ptevac_prev)
                nkt = 2 * (j + 1)
                for kt in range(nkt):
                    tgt = (PTa if kt < 8 else PTb)
                    f = (lambda e, kt=kt, tgt=tgt:
                         e.transpose(tgt[:, (kt % 8) * 128:(kt % 8) * 128 + 128],
                                     Pbs[:, kt * 128:(kt + 1) * 128], ident[:]))
                    if kt == min(nkt, 8) - 1 or kt == nkt - 1:
                        I("tensor", eT, f)
                    else:
                        N("tensor", f)
                trj_tok = eT.n
                trj_prev = trj_tok
                W("vector", eT, trj_tok)
                if avj_prev:
                    W("vector", eT, avj_prev)
                na = min(nkt, 8)
                I("vector", eV, lambda e, na=na: e.tensor_copy(
                    Pt[:, 0:na * 128], PTa[:, 0:na * 128]))
                if nkt > 8:
                    I("vector", eV, lambda e, nkt=nkt: e.tensor_copy(
                        Pt[:, 1024:nkt * 128], PTb[:, 0:(nkt - 8) * 128]))
                ptevac_prev = eV.n
                W("tensor", eV, ptevac_prev)
                ob = (j % 2) * 512
                need("tensor", f"opd{ob}")
                for m in range(nkt):
                    f = (lambda e, m=m, ob=ob, nkt=nkt, h=h:
                         e.matmul(OPp[:, ob: ob + 128],
                                  Vb2[h % 2][:, m * 128:(m + 1) * 128],
                                  Pt[:, m * 128:(m + 1) * 128],
                                  start=(m == 0), stop=(m == nkt - 1)))
                    if m == nkt - 1:
                        I("tensor", eT, f)
                    else:
                        N("tensor", f)
                avj_tok = eT.n
                avj_prev = avj_tok
                W("vector", eT, avj_tok)
                I("vector", eV, lambda e, h=h, j=j, ob=ob: e.tensor_copy(
                    Oattn[:, h * 1024 + j * 128: h * 1024 + j * 128 + 128],
                    OPp[:, ob: ob + 128]))
                o_evac_tok[(h, j)] = eV.n
                used(f"opd{ob}", eV)
                if h + 1 < NH:
                    proj_unit_evac(h + 1, j, *u_mm)
                    if u_mm2 is not None:
                        proj_unit_evac(h + 1, 15 - j, *u_mm2)
            used(f"kt{h % 2}", eT)
            used(f"vb{h % 2}", eT)
        last_xbkv_mm = last_xbkv_mm[0]

        attn_done_eV = eV.n
        attn_done_eT = eT.n
        tap("kt15", KT2[1][:, :], eS, eS.n)
        tap("v15", Vb2[1][:, :], eV, eV.n)
        tap("p15_7", Pbs[:, :], eS, eS.n)
        tap("oattn", Oattn[:, :], eV, attn_done_eV)

        def ln_stats_chunk(src_ap, tt, c, src_sem, src_tok2, sums_row, sq_row_ap, first, last):
            W("scalar", src_sem, src_tok2)
            need("scalar", f"gsb{c % 2}")
            need("scalar", f"mrb{c % 2}")
            I("scalar", eS, lambda e, c=c: e.activation(
                tmpb[c % 2], src_ap[:, c * 1024 + tt * 512: c * 1024 + tt * 512 + 512],
                AF.Identity))
            I("scalar", eS, lambda e, c=c: e.activation(
                tmps[c % 2], src_ap[:, c * 1024 + tt * 512: c * 1024 + tt * 512 + 512],
                AF.Square))
            ct = eS.n
            W("tensor", eS, ct)
            if first:
                need("tensor", "SBall")
            N("tensor", lambda e, c=c, sums_row=sums_row: e.matmul(
                sums_row, onescol[:], tmpb[c % 2], start=first, stop=last))
            f = (lambda e, c=c, sq_row_ap=sq_row_ap: e.matmul(
                sq_row_ap, onescol[:], tmps[c % 2], start=first, stop=last))
            I("tensor", eT, f)
            used(f"gsb{c % 2}", eT)
            used(f"mrb{c % 2}", eT)
            return eT.n

        def ln_finalize_apply(src_ap, dst32_ap, dstb_ap, tt, stat_last, sums_row, sq_row_ap,
                              src_tok, dst_wait):
            W("vector", eT, stat_last)
            I("vector", eV, lambda e, sums_row=sums_row: e.tensor_scalar_mul(m_row[:], sums_row, 1.0 / D))
            I("vector", eV, lambda e, sq_row_ap=sq_row_ap: e.tensor_scalar_mul(msq_row[:], sq_row_ap, 1.0 / D))
            t1 = eV.n
            used("SBall", eV)
            W("scalar", eV, t1)
            I("scalar", eS, lambda e: e.activation(m2_row[:], m_row[:], AF.Square))
            t2 = eS.n
            W("vector", eS, t2)
            I("vector", eV, lambda e: e.tensor_tensor(msq_row[:], msq_row[:], m2_row[:], OP.subtract))
            t3 = eV.n
            W("scalar", eV, t3)
            I("scalar", eS, lambda e: e.activation(m2_row[:], msq_row[:], AF.Sqrt, bias=c_eps[:]))
            t4 = eS.n
            W("vector", eS, t4)
            N("vector", lambda e: e.reciprocal(r_row[:], m2_row[:]))
            I("vector", eV, lambda e: e.drain())
            t5 = eV.n
            W("sync", eV, t5)
            need("sync", "mrb0")
            W("sync", dB, dB.n)
            I("sync", dB, lambda e: e.dma_start(
                mbc[:], bass.AP(m_row, 0, [[512, 1], [0, 128], [1, 512]])), 16)
            need("sync", "mrb1")
            W("sync", dB, dB.n)
            I("sync", dB, lambda e: e.dma_start(
                rbc[:], bass.AP(r_row, 0, [[512, 1], [0, 128], [1, 512]])), 16)
            bc_tok = dB.n
            W("vector", dB, bc_tok)
            for s, v in dst_wait:
                W("vector", s, v)
            mul_tok = {}
            castb_tok = {}
            for c in range(NDC):
                lo = c * 1024 + tt * 512
                s2, v2 = src_tok[c]
                W("vector", s2, v2)
                I("vector", eV, lambda e, lo=lo: e.tensor_tensor(
                    dst32_ap[:, lo:lo + 512], src_ap[:, lo:lo + 512], mbc[:], OP.subtract))
            I("vector", eV, lambda e: e.drain())
            for c in range(NDC):
                lo = c * 1024 + tt * 512
                I("vector", eV, lambda e, lo=lo: e.tensor_tensor(
                    dst32_ap[:, lo:lo + 512], dst32_ap[:, lo:lo + 512], rbc[:], OP.mult))
                mul_tok[c] = eV.n
                if dstb_ap is not None:
                    W("scalar", eV, mul_tok[c])
                    I("scalar", eS, lambda e, lo=lo: e.activation(
                        dstb_ap[:, lo:lo + 512], dst32_ap[:, lo:lo + 512], AF.Identity))
                    castb_tok[c] = eS.n
            used("mrb0", eV)
            used("mrb1", eV)
            return mul_tok, castb_tok

        W("gpsimd", eT, last_xbkv_mm)
        W("gpsimd", dG, dG.n)
        I("gpsimd", dG, lambda e: e.dma_start(
            XownA.rearrange("p (c t) -> p c t", c=16),
            xq_d.ap().rearrange("(c p) t -> p c t", p=128)), 16)
        xown_dtok = dG.n
        I("gpsimd", eG, lambda e: e.drain())
        xown_tok = eG.n

        stt_tok = {}
        W("tensor", eV, sb_read_tok[0])
        W("tensor", eV, sb_read_tok[1])
        W("tensor", eS, unit_evac[0])
        W("tensor", eS, unit_evac[1])
        for oc in range(16):
            need("gpsimd", f"wbf{oc % 4}")
            W("gpsimd", dG, dG.n)
            I("gpsimd", dG, lambda e, oc=oc: e.dma_start(
                wbf[oc % 4][:], wout_d.ap()[oc].rearrange("p c f -> p (c f)")), 16)
            I("gpsimd", eG, lambda e: e.drain())
            W("tensor", dG, dG.n)
            W("tensor", eG, eG.n)
            if oc == 0:
                W("tensor", eV, attn_done_eV)
            if oc >= 2:
                W("tensor", eV, stt_tok[(oc - 2, 1)])
            for ac in range(16):
                for th in range(2):
                    bank = (oc % 2) * 2 + th
                    f = (lambda e, oc=oc, ac=ac, th=th, bank=bank:
                         e.matmul(SB[:, bank * 512:(bank + 1) * 512],
                                  wbf[oc % 4][:, ac * 128:(ac + 1) * 128],
                                  Oattn[:, ac * 1024 + th * 512: ac * 1024 + th * 512 + 512],
                                  start=(ac == 0), stop=(ac == 15)))
                    if ac == 15 and th == 1:
                        I("tensor", eT, f)
                    else:
                        N("tensor", f)
            used(f"wbf{oc % 4}", eT)
            oproj_tok = eT.n
            W("vector", eT, oproj_tok)
            if oc == 0:
                W("vector", dG, xown_dtok)
                W("vector", eG, xown_tok)
            for th in range(2):
                bank = (oc % 2) * 2 + th
                I("vector", eV, lambda e, oc=oc, th=th, bank=bank:
                  e.scalar_tensor_tensor(
                      XownA[:, oc * 1024 + th * 512: oc * 1024 + th * 512 + 512],
                      SB[:, bank * 512:(bank + 1) * 512],
                      bout_s[:, oc:oc + 1],
                      XownA[:, oc * 1024 + th * 512: oc * 1024 + th * 512 + 512],
                      OP.add, OP.add))
                stt_tok[(oc, th)] = eV.n
        oproj_all_eT = eT.n
        need("tensor", "opd0")
        need("tensor", "opd512")
        for cs in range(16):
            st1 = ln_stats_chunk(XownA, 0, cs, eV, stt_tok[(cs, 0)],
                                 OPp[0:1, 0:512], OPp[0:1, 512:1024],
                                 first=(cs == 0), last=(cs == 15))
        ln1_statlast_0 = st1
        used("SBall", eV)

        t_resA = tap("resA", XownA[:, :], eV, eV.n)
        if t_resA is not None:
            W("scalar", dO, t_resA)
        ln1_castb = {}
        src_tok0 = {c: (eV, stt_tok[(c, 0)]) for c in range(NDC)}
        mul_tok, cb0 = ln_finalize_apply(XownA, XN1, XN1b, 0, ln1_statlast_0,
                                         OPp[0:1, 0:512], OPp[0:1, 512:1024],
                                         src_tok0, [(eT, oproj_all_eT)])
        used("opd0", eV)
        used("opd512", eV)
        ln1_castb[0] = cb0
        need("tensor", "opd0")
        need("tensor", "opd512")
        for cs in range(16):
            st2 = ln_stats_chunk(XownA, 1, cs, eV, stt_tok[(cs, 1)],
                                 OPp[0:1, 0:512], OPp[0:1, 512:1024],
                                 first=(cs == 0), last=(cs == 15))
        src_tok1 = {c: (eV, stt_tok[(c, 1)]) for c in range(NDC)}
        mul_tok, cb1 = ln_finalize_apply(XownA, XN1, XN1b, 1, st2,
                                         OPp[0:1, 0:512], OPp[0:1, 512:1024],
                                         src_tok1, [])
        used("opd0", eV)
        used("opd512", eV)
        ln1_castb[1] = cb1
        xn1b_done = eS.n
        t_xn1 = tap("xn1", XN1[:, :], eS, xn1b_done)
        if t_xn1 is not None:
            W("vector", dO, t_xn1)

        dn_evac_tok = {}
        for tt in range(NTT):
            htok = {}
            gu_state = {}

            def gu_load(ic, tt=tt):
                sg_, su_ = 2 * (ic % 2), 2 * (ic % 2) + 1
                sb4 = ic % 2
                need("sync", f"stg4{sb4}a")
                need("sync", f"stg4{sb4}b")
                W("sync", dW, dW.n)
                I("sync", dW, lambda e, ic=ic, sb4=sb4: e.dma_start(
                    stg4[sb4][:], wgu_d.ap()[ic]), 16)
                gu_dma = dW.n
                W("scalar", dW, gu_dma)
                need("scalar", f"wbf{sg_}")
                I("scalar", eS, lambda e, sg_=sg_, sb4=sb4: e.activation(
                    wbf[sg_][:], stg4[sb4][:, 0:2048], AF.Identity, bias=c128[:]))
                gcast = eS.n
                used(f"stg4{sb4}a", eS)
                W("vector", dW, gu_dma)
                need("vector", f"wbf{su_}")
                I("vector", eV, lambda e, su_=su_, sb4=sb4: e.tensor_scalar_add(
                    wbf[su_][:], stg4[sb4][:, 2048:4096], -128.0))
                ucast = eV.n
                used(f"stg4{sb4}b", eV)
                gu_state[ic] = (sg_, su_, gcast, ucast)

            def gu_compute(ic, tt=tt):
                sg_, su_, gcast, ucast = gu_state[ic]
                gb = (ic % 2) * 512
                ub = 1024 + (ic % 2) * 512
                W("tensor", eS, gcast)
                W("tensor", eV, ucast)
                if ic == 0:
                    need("tensor", "SBall")
                if ic >= 2:
                    W("tensor", eV, htok[ic - 2])
                for dc in range(NDC):
                    if ic == 0:
                        W("tensor", eS, ln1_castb[tt][dc])
                    rr = XN1b[:, dc * 1024 + tt * 512: dc * 1024 + tt * 512 + 512]
                    N("tensor", lambda e, sg_=sg_, dc=dc, gb=gb, rr=rr:
                      e.matmul(SB[:, gb:gb + 512], wbf[sg_][:, dc * 128:(dc + 1) * 128],
                               rr, start=(dc == 0), stop=(dc == NDC - 1)))
                    f = (lambda e, su_=su_, dc=dc, ub=ub, rr=rr:
                         e.matmul(SB[:, ub:ub + 512], wbf[su_][:, dc * 128:(dc + 1) * 128],
                                  rr, start=(dc == 0), stop=(dc == NDC - 1)))
                    if dc == NDC - 1:
                        I("tensor", eT, f)
                    else:
                        N("tensor", f)
                used(f"wbf{sg_}", eT)
                used(f"wbf{su_}", eT)
                gu_tok = eT.n
                W("scalar", eT, gu_tok)
                need("scalar", f"gsb{ic % 2}")
                I("scalar", eS, lambda e, ic=ic, gb=gb: e.activation(
                    gs_buf[ic % 2][:], SB[:, gb:gb + 512], AF.Silu,
                    scale=gsc_s[:, ic:ic + 1]))
                gs_tok = eS.n
                W("vector", eS, gs_tok)
                I("vector", eV, lambda e, ic=ic, ub=ub: e.scalar_tensor_tensor(
                    Hbuf[:, ic * 512:(ic + 1) * 512], SB[:, ub:ub + 512],
                    usc_s[:, ic:ic + 1], gs_buf[ic % 2][:], OP.mult, OP.mult))
                htok[ic] = eV.n
                used(f"gsb{ic % 2}", eV)
                if ic == NIC - 1:
                    used("SBall", eV)

            for ic in range(NIC + 1):
                if ic < NIC:
                    gu_load(ic)
                if ic >= 1:
                    gu_compute(ic - 1)

            if tt == 0:
                t_h0 = tap("h0", Hbuf[:, :], eV, eV.n)
                if t_h0 is not None:
                    W("tensor", dO, t_h0)
            dn_state = {}

            def dn_load(hc):
                toks = []
                for half in range(2):
                    need("sync", f"stg4{half}a")
                    need("sync", f"stg4{half}b")
                    W("sync", dW, dW.n)
                    I("sync", dW, lambda e, hc=hc, half=half: e.dma_start(
                        stg4[half][:],
                        wd_d.ap()[hc][:, half * 32:(half + 1) * 32, :].rearrange("p c f -> p (c f)")), 16)
                    h_dma = dW.n
                    for qq in range(2):
                        q = half * 2 + qq
                        eng = "scalar" if q % 2 == 0 else "vector"
                        W(eng, dW, h_dma)
                        need(eng, f"wbf{q}")
                        if eng == "scalar":
                            I("scalar", eS, lambda e, half=half, qq=qq, q=q: e.activation(
                                wbf[q][:], stg4[half][:, qq * 2048:(qq + 1) * 2048],
                                AF.Identity, bias=c128[:]))
                            toks.append((eS, eS.n))
                            used(f"stg4{half}a", eS)
                        else:
                            I("vector", eV, lambda e, half=half, qq=qq, q=q: e.tensor_scalar_add(
                                wbf[q][:], stg4[half][:, qq * 2048:(qq + 1) * 2048], -128.0))
                            toks.append((eV, eV.n))
                            used(f"stg4{half}b", eV)
                dn_state[hc] = toks

            def dn_compute(hc, tt=tt):
                wcast_tok = dn_state[hc]
                ob = (hc % 2) * 512
                need("tensor", f"opd{ob}")
                W("tensor", eV, htok[NIC - 1])
                for q in range(4):
                    s, v = wcast_tok[q]
                    W("tensor", s, v)
                for ic in range(NIC):
                    q = ic // 16
                    f = (lambda e, ic=ic, q=q, ob=ob:
                         e.matmul(OPp[:, ob: ob + 512],
                                  wbf[q % 4][:, (ic % 16) * 128:(ic % 16) * 128 + 128],
                                  Hbuf[:, ic * 512:(ic + 1) * 512],
                                  start=(ic == 0), stop=(ic == NIC - 1)))
                    if ic % 16 == 15:
                        I("tensor", eT, f)
                        used(f"wbf{q}", eT)
                    else:
                        N("tensor", f)
                dn_tok = eT.n
                W("vector", eT, dn_tok)
                lo = hc * 1024 + tt * 512
                I("vector", eV, lambda e, hc=hc, lo=lo, ob=ob: e.scalar_tensor_tensor(
                    XN1[:, lo:lo + 512], OPp[:, ob:ob + 512], dsc_s[:, hc:hc + 1],
                    XN1[:, lo:lo + 512], OP.mult, OP.add))
                dn_evac_tok[(tt, hc)] = eV.n
                used(f"opd{ob}", eV)

            for hc in range(16):
                dn_load(hc)
                dn_compute(hc)
            for hc in range(16):
                ln2_statlast = ln_stats_chunk(XN1, tt, hc, eV, dn_evac_tok[(tt, hc)],
                                              SB[0:1, 0:512], SB[0:1, 512:1024],
                                              first=(hc == 0), last=(hc == 15))

            if tt == 0:
                t_yd = tap("ydown", XN1[:, :], eV, eV.n)
                if t_yd is not None:
                    W("vector", dO, t_yd)
            src_tok = {c: (eV, dn_evac_tok[(tt, c)]) for c in range(NDC)}
            mul_tok, _ = ln_finalize_apply(XN1, XN1, None, tt, ln2_statlast,
                                           SB[0:1, 0:512], SB[0:1, 512:1024],
                                           src_tok, [])
            for c in range(NDC):
                W("gpsimd", eV, mul_tok[c])
                I("gpsimd", dO, lambda e, c=c, tt=tt: e.dma_start(
                    yt_d.ap()[c * 128:(c + 1) * 128, tt * 512:(tt + 1) * 512],
                    XN1[:, c * 1024 + tt * 512: c * 1024 + tt * 512 + 512]), 16)

        W("gpsimd", dO, dO.n)

        @block.sync
        def _(e):
            for eng, f in plan:
                if eng == "sync":
                    f(e)

        @block.gpsimd
        def _(e):
            for eng, f in plan:
                if eng == "gpsimd":
                    f(e)

        @block.scalar
        def _(e):
            for eng, f in plan:
                if eng == "scalar":
                    f(e)

        @block.vector
        def _(e):
            for eng, f in plan:
                if eng == "vector":
                    f(e)

        @block.tensor
        def _(e):
            for eng, f in plan:
                if eng == "tensor":
                    f(e)

    return nc


def kernel(L0_attn_x, L0_attn_w_qkv, L0_attn_b_qkv, L0_attn_w_out, L0_attn_b_out,
           L0_ffn_gate_w, L0_ffn_gate_scale, L0_ffn_up_w, L0_ffn_up_scale,
           L0_ffn_down_w, L0_ffn_down_scale):
    x = np.asarray(L0_attn_x, np.float32)
    wqkv = np.asarray(L0_attn_w_qkv, np.float32)
    bqkv = np.asarray(L0_attn_b_qkv, np.float32)
    wout = np.asarray(L0_attn_w_out, np.float32)
    bout = np.asarray(L0_attn_b_out, np.float32)
    gw = np.asarray(L0_ffn_gate_w)
    uw = np.asarray(L0_ffn_up_w)
    dw = np.asarray(L0_ffn_down_w)
    gsc = np.asarray(L0_ffn_gate_scale, np.float32)
    usc = np.asarray(L0_ffn_up_scale, np.float32)
    dsc = np.asarray(L0_ffn_down_scale, np.float32)

    wqkv_s = np.ascontiguousarray(wqkv.reshape(48, 128, 16, 128).transpose(0, 3, 2, 1))
    wout_s = np.ascontiguousarray(wout.reshape(16, 128, 16, 128).transpose(0, 3, 2, 1))
    wg_s = gw.astype(np.uint8).reshape(64, 128, 16, 128).transpose(0, 3, 2, 1).reshape(64, 128, 2048)
    wu_s = uw.astype(np.uint8).reshape(64, 128, 16, 128).transpose(0, 3, 2, 1).reshape(64, 128, 2048)
    wgu_s = np.ascontiguousarray(np.concatenate([wg_s, wu_s], axis=2))
    wd_s = np.ascontiguousarray(dw.astype(np.uint8).reshape(16, 128, 64, 128).transpose(0, 3, 2, 1))
    bqkv_s = np.ascontiguousarray(bqkv.reshape(48, 128).T)
    bout_s = np.ascontiguousarray(bout.reshape(16, 128).T)
    gsc_s = np.ascontiguousarray(gsc.reshape(64, 128).T)
    usc_s = np.ascontiguousarray(usc.reshape(64, 128).T)
    dsc_s = np.ascontiguousarray(dsc.reshape(16, 128).T)
    ident = np.eye(128, dtype=ml_dtypes.bfloat16)

    tri = np.triu(np.full((128, 128), -1e9, np.float32), k=1)
    in_maps = []
    for core in range(8):
        b, par = core // 2, core % 2
        xb = x[b]
        order = []
        for t in range(8):
            order += [2 * t + par, 2 * t + 1 - par]
        xkv = np.ascontiguousarray(
            np.concatenate([xb[g * 128:(g + 1) * 128] for g in order], axis=0).T)
        xq = np.ascontiguousarray(
            np.concatenate([xb[(2 * t + par) * 128:(2 * t + par + 1) * 128]
                            for t in range(8)], axis=0).T)
        mask = np.zeros((128, 256), np.float32)
        mask[:, 0:128] = tri
        if par == 0:
            mask[:, 128:256] = -1e9
        in_maps.append(dict(
            xkv=xkv, xq=xq, mask=mask, identw=ident,
            wqkv=wqkv_s, wout=wout_s, wgu=wgu_s, wd=wd_s,
            bqkv=bqkv_s, bout=bout_s, gsc=gsc_s, usc=usc_s, dsc=dsc_s))

    nc = build_nc()
    res = run_bass_kernel_spmd(nc, in_maps, core_ids=list(range(8)))

    out = np.empty((4, 2048, 2048), np.float32)
    for core in range(8):
        b, par = core // 2, core % 2
        yt = np.asarray(res.results[core]["yt"], np.float32)
        for j in range(8):
            g = 2 * j + par
            out[b, g * 128:(g + 1) * 128, :] = yt[:, j * 128:(j + 1) * 128].T
    return out


# version 21
# speedup vs baseline: 1.0599x; 1.0599x over previous
import numpy as np
import ml_dtypes
from contextlib import ExitStack

import concourse.bass as bass
import concourse.mybir as mybir
from concourse.bass_utils import run_bass_kernel_spmd

F32 = mybir.dt.float32
BF16 = mybir.dt.bfloat16
U8 = mybir.dt.uint8
AF = mybir.ActivationFunctionType
OP = mybir.AluOpType
AX = mybir.AxisListType.X

D = 2048
NDC = 16
NH = 16
HD = 128
TQ = 1024
TKV = 2048
NQT = 8
NKT = 16
II = 8192
NIC = 64
TT = 512
NTT = 2
QK_SCALE = 1.0 / float(np.sqrt(HD))


class Sem:
    def __init__(self, name):
        self.name = name
        self.h = None
        self.n = 0


def build_nc(taps=()):
    nc = bass.Bass("TRN2", target_bir_lowering=False)

    xkv_d = nc.dram_tensor("xkv", [D, TKV], F32, kind="ExternalInput")
    xq_d = nc.dram_tensor("xq", [D, TQ], F32, kind="ExternalInput")
    wqkv_d = nc.dram_tensor("wqkv", [48, 128, NDC, 128], F32, kind="ExternalInput")
    wout_d = nc.dram_tensor("wout", [16, 128, NDC, 128], F32, kind="ExternalInput")
    wgu_d = nc.dram_tensor("wgu", [NIC, 128, 4096], U8, kind="ExternalInput")
    wd_d = nc.dram_tensor("wd", [16, 128, NIC, 128], U8, kind="ExternalInput")
    bqkv_d = nc.dram_tensor("bqkv", [128, 48], F32, kind="ExternalInput")
    bout_d = nc.dram_tensor("bout", [128, 16], F32, kind="ExternalInput")
    gsc_d = nc.dram_tensor("gsc", [128, 64], F32, kind="ExternalInput")
    usc_d = nc.dram_tensor("usc", [128, 64], F32, kind="ExternalInput")
    dsc_d = nc.dram_tensor("dsc", [128, 16], F32, kind="ExternalInput")
    mask_d = nc.dram_tensor("mask", [128, 256], F32, kind="ExternalInput")
    ident_d = nc.dram_tensor("identw", [128, 128], BF16, kind="ExternalInput")
    yt_d = nc.dram_tensor("yt", [D, TQ], F32, kind="ExternalOutput")
    tap_shapes = {
        "qt": [128, 16384], "kt15": [128, 2048], "v15": [128, 2048],
        "oattn": [128, 16384], "resA": [128, 16384], "xn1": [128, 16384],
        "h0": [128, 32768], "p15_7": [128, 2048], "ydown": [128, 16384],
    }
    tap_d = {t: nc.dram_tensor("tap_" + t, [128, tap_shapes[t][1]], F32,
                               kind="ExternalOutput") for t in taps}

    plan = []

    def E(eng, f):
        plan.append((eng, f))

    sems = [Sem(s) for s in
            ("dC", "dW", "dG", "dO", "dB", "eT", "eS", "eV", "eG")]
    dC, dW, dG, dO, dB, eT, eS, eV, eG = sems

    _wait_hwm = {}

    def W(eng, sem, val):
        assert val <= sem.n, (eng, sem.name, val, sem.n)
        if val <= 0:
            return
        key = (eng, sem.name)
        if _wait_hwm.get(key, -1) >= val:
            return
        _wait_hwm[key] = val
        E(eng, lambda e, s=sem, v=val: e.wait_ge(s.h, v))

    def I(eng, sem, f, k=1):
        sem.n += k
        E(eng, lambda e, f=f, s=sem, k=k: f(e).then_inc(s.h, k))
        return sem.n

    def N(eng, f):
        E(eng, lambda e, f=f: f(e))

    with ExitStack() as es:
        ent = es.enter_context
        AXa = ent(nc.sbuf_tensor("AXa", [128, 65536], U8))
        AZa = ent(nc.sbuf_tensor("AZa", [128, 65536], U8))
        AWa = ent(nc.sbuf_tensor("AWa", [128, 32768], U8))
        wbf = [ent(nc.sbuf_tensor(f"wbf{i}", [128, 2048], BF16)) for i in range(4)]
        stg = [ent(nc.sbuf_tensor(f"stg{i}", [128, 2048], U8)) for i in range(2)]
        stg4 = [ent(nc.sbuf_tensor(f"stg4{i}", [128, 4096], U8)) for i in range(2)]
        bqkv_s = ent(nc.sbuf_tensor("bqkv_s", [128, 48], F32))
        bqs = ent(nc.sbuf_tensor("bqs", [128, 16], F32))
        bout_s = ent(nc.sbuf_tensor("bout_s", [128, 16], F32))
        gsc_s = ent(nc.sbuf_tensor("gsc_s", [128, 64], F32))
        usc_s = ent(nc.sbuf_tensor("usc_s", [128, 64], F32))
        dsc_s = ent(nc.sbuf_tensor("dsc_s", [128, 16], F32))
        mask_s = ent(nc.sbuf_tensor("mask_s", [128, 256], F32))
        ident = ent(nc.sbuf_tensor("ident", [128, 128], BF16))
        onescol = ent(nc.sbuf_tensor("onescol", [128, 1], BF16))
        c128 = ent(nc.sbuf_tensor("c128", [128, 1], F32))
        c_eps = ent(nc.sbuf_tensor("c_eps", [1, 1], F32))
        negm = ent(nc.sbuf_tensor("negm", [128, 1], F32))
        sumc = ent(nc.sbuf_tensor("sumc", [128, 4], F32))
        sums = ent(nc.sbuf_tensor("sums", [128, 1], F32))
        rvec = ent(nc.sbuf_tensor("rvec", [128, 1], F32))
        scr4 = ent(nc.sbuf_tensor("scr4", [128, 4], F32))
        mx4 = ent(nc.sbuf_tensor("mx4", [128, 4], F32))
        gs_buf = [ent(nc.sbuf_tensor(f"gsb{i}", [128, 512], F32)) for i in range(2)]
        m_row = ent(nc.sbuf_tensor("m_row", [1, 512], F32))
        msq_row = ent(nc.sbuf_tensor("msq_row", [1, 512], F32))
        m2_row = ent(nc.sbuf_tensor("m2_row", [1, 512], F32))
        r_row = ent(nc.sbuf_tensor("r_row", [1, 512], F32))
        mbc = ent(nc.sbuf_tensor("mbc", [128, 512], F32))
        rbc = ent(nc.sbuf_tensor("rbc", [128, 512], F32))
        tmpb = [gs_buf[0][:].bitcast(BF16)[:, 0:512], gs_buf[1][:].bitcast(BF16)[:, 0:512]]
        tmps = [mbc[:].bitcast(BF16)[:, 0:512], rbc[:].bitcast(BF16)[:, 0:512]]
        SB = ent(nc.psum_tensor("SB", [128, 2048], F32))
        PTa = ent(nc.psum_tensor("PTa", [128, 1024], BF16))
        PTb = ent(nc.psum_tensor("PTb", [128, 1024], BF16))
        OPp = ent(nc.psum_tensor("OPp", [128, 1024], F32))
        for s in sems:
            s.h = ent(nc.semaphore(s.name))
        block = ent(nc.Block())

        def reg(arena, off_bytes, n_elems, dt):
            esz = {F32: 4, BF16: 2, U8: 1}[dt]
            return arena[:, off_bytes: off_bytes + n_elems * esz].bitcast(dt)

        Xbkv = reg(AXa, 0, 32768, BF16)
        XownA = reg(AXa, 0, 16384, F32)
        Hbuf = reg(AXa, 0, 32768, BF16)
        QT = reg(AZa, 0, 16384, BF16)
        Oattn = reg(AZa, 32768, 16384, BF16)
        XN1 = reg(AZa, 0, 16384, F32)
        KT2 = [reg(AWa, 0, 2048, BF16), reg(AWa, 4096, 2048, BF16)]
        Vb2 = [reg(AWa, 8192, 2048, BF16), reg(AWa, 12288, 2048, BF16)]
        Ssb = reg(AWa, 16384, 2048, F32)
        Pbs = reg(AWa, 24576, 2048, BF16)
        Pt = reg(AWa, 28672, 2048, BF16)
        XN1b = reg(AWa, 0, 16384, BF16)

        buf_tok = {}

        def need(eng, key):
            if key in buf_tok:
                s, v = buf_tok[key]
                W(eng, s, v)

        def used(key, sem):
            buf_tok[key] = (sem, sem.n)

        def tap(name, src_ap, wsem, wtok):
            if name not in tap_d:
                return None
            W("gpsimd", wsem, wtok)
            I("gpsimd", dO, lambda e, name=name, src_ap=src_ap: e.dma_start(
                tap_d[name].ap(), src_ap), 16)
            return dO.n

        for dst, src in ((bqkv_s, bqkv_d), (bout_s, bout_d), (gsc_s, gsc_d),
                         (usc_s, usc_d), (dsc_s, dsc_d), (mask_s, mask_d),
                         (ident, ident_d)):
            I("sync", dC, lambda e, d=dst, s=src: e.dma_start(d[:], s.ap()), 16)
        tok_dC = dC.n

        N("vector", lambda e: e.memset(onescol[:], 1.0))
        N("vector", lambda e: e.memset(c128[:], -128.0))
        N("vector", lambda e: e.memset(c_eps[:], 1e-5))
        W("vector", dC, tok_dC)
        tok_memset = I("vector", eV, lambda e: e.tensor_scalar_mul(
            bqs[:], bqkv_s[:, 0:16], QK_SCALE))

        xkv_tok = []
        for dc in range(NDC):
            I("gpsimd", dG, lambda e, dc=dc: e.dma_start(
                Xbkv[:, dc * 2048:(dc + 1) * 2048],
                xkv_d.ap()[dc * 128:(dc + 1) * 128, :]), 16)
            xkv_tok.append(dG.n)
        I("gpsimd", eG, lambda e: e.drain())
        xcast_done = eG.n

        W("tensor", dC, tok_dC)
        W("scalar", dC, tok_dC)
        W("scalar", eV, tok_memset)
        for fc in range(NH):
            need("gpsimd", f"wbf{fc % 4}")
            W("gpsimd", dG, dG.n)
            I("gpsimd", dG, lambda e, fc=fc: e.dma_start(
                wbf[fc % 4][:],
                wqkv_d.ap()[fc].rearrange("p c f -> p (c f)")), 16)
            I("gpsimd", eG, lambda e: e.drain())
            W("tensor", dG, dG.n)
            W("tensor", eG, eG.n)
            if fc >= 2:
                W("tensor", eS, ev_tok[fc - 2])
            if fc == 0:
                W("tensor", eV, tok_memset)
            for dc in range(NDC):
                for th in range(2):
                    bank = (fc % 2) * 2 + th
                    rhs = Xbkv[:, dc * 2048:(dc + 1) * 2048].rearrange(
                        "p (a b f) -> p a b f", a=8, b=2)[:, th * 4:(th + 1) * 4, 0:1, :]
                    f = (lambda e, fc=fc, dc=dc, bank=bank, rhs=rhs:
                         e.matmul(SB[:, bank * 512:(bank + 1) * 512],
                                  wbf[fc % 4][:, dc * 128:(dc + 1) * 128], rhs,
                                  start=(dc == 0), stop=(dc == NDC - 1)))
                    if dc == NDC - 1 and th == 1:
                        I("tensor", eT, f)
                    else:
                        N("tensor", f)
            used(f"wbf{fc % 4}", eT)
            mm_tok = eT.n
            if fc == 0:
                ev_tok = {}
            W("scalar", eT, mm_tok)
            for th in range(2):
                bank = (fc % 2) * 2 + th
                I("scalar", eS, lambda e, fc=fc, th=th, bank=bank: e.activation(
                    QT[:, fc * 1024 + th * 512: fc * 1024 + th * 512 + 512],
                    SB[:, bank * 512:(bank + 1) * 512],
                    AF.Identity, bias=bqs[:, fc:fc + 1], scale=QK_SCALE))
            ev_tok[fc] = eS.n
        q_evac_done = eS.n
        tap("qt", QT[:, :], eS, q_evac_done)
        last_xbkv_mm = [None]

        VtS = [stg[0][:].bitcast(BF16), stg[1][:].bitcast(BF16)]
        o_evac_tok = {}
        sb_read_tok = [0, 0]
        unit_evac = [q_evac_done, q_evac_done]
        pscale_prev = 0
        trj_prev = 0
        avj_prev = 0
        ptevac_prev = 0
        prev_exp = 0
        kvinfo = {}
        kt_ready = {}
        vt_ready = {}

        def kv_dma(hh):
            kb, vb = (2 * hh) % 4, (2 * hh + 1) % 4
            need("gpsimd", f"wbf{kb}")
            W("gpsimd", dG, dG.n)
            I("gpsimd", dG, lambda e, hh=hh, kb=kb: e.dma_start(
                wbf[kb][:], wqkv_d.ap()[16 + hh].rearrange("p c f -> p (c f)")), 16)
            kd = dG.n
            I("gpsimd", eG, lambda e: e.drain())
            kt_ = eG.n
            need("gpsimd", f"wbf{vb}")
            W("gpsimd", dG, dG.n)
            I("gpsimd", dG, lambda e, hh=hh, vb=vb: e.dma_start(
                wbf[vb][:], wqkv_d.ap()[32 + hh].rearrange("p c f -> p (c f)")), 16)
            vd = dG.n
            I("gpsimd", eG, lambda e: e.drain())
            vt_ = eG.n
            kvinfo[hh] = (kb, vb, kd, kt_, vd, vt_)
            kt_ready[hh] = [None] * 4

        def proj_unit_mm(hh, unit):
            kb, vb, kd, kt_, vd, vt_ = kvinfo[hh]
            b = 2 + (unit % 2)
            is_v = unit < 4
            tc = unit if is_v else unit - 4
            wb_i = vb if is_v else kb
            if is_v:
                W("tensor", dG, vd)
                W("tensor", eG, vt_)
            else:
                W("tensor", dG, kd)
                W("tensor", eG, kt_)
            W("tensor", eS, unit_evac[unit % 2])
            for dc in range(NDC):
                f = (lambda e, wb_i=wb_i, dc=dc, b=b, tc=tc:
                     e.matmul(SB[:, b * 512:(b + 1) * 512],
                              wbf[wb_i][:, dc * 128:(dc + 1) * 128],
                              Xbkv[:, dc * 2048 + tc * 512: dc * 2048 + tc * 512 + 512],
                              start=(dc == 0), stop=(dc == NDC - 1)))
                if dc == NDC - 1:
                    I("tensor", eT, f)
                else:
                    N("tensor", f)
            last_xbkv_mm[0] = eT.n
            if unit == 3:
                used(f"wbf{vb}", eT)
            if unit == 7:
                used(f"wbf{kb}", eT)
            return eT.n, b, is_v, tc

        def proj_unit_evac(hh, unit, mm_tok, b, is_v, tc):
            W("scalar", eT, mm_tok)
            if is_v:
                need("scalar", f"stg{tc // 2}")
                I("scalar", eS, lambda e, hh=hh, b=b, tc=tc: e.activation(
                    VtS[tc // 2][:, (tc % 2) * 512:(tc % 2) * 512 + 512],
                    SB[:, b * 512:(b + 1) * 512],
                    AF.Identity, bias=bqkv_s[:, 32 + hh:33 + hh]))
                vt_ready[hh] = eS.n
            else:
                need("scalar", f"kt{hh % 2}")
                I("scalar", eS, lambda e, hh=hh, b=b, tc=tc: e.activation(
                    KT2[hh % 2][:, tc * 512:(tc + 1) * 512],
                    SB[:, b * 512:(b + 1) * 512],
                    AF.Identity, bias=bqkv_s[:, 16 + hh:17 + hh]))
                kt_ready[hh][tc] = eS.n
            unit_evac[unit % 2] = eS.n

        def v_transposes(hh):
            W("tensor", eS, vt_ready[hh])
            if ptevac_prev:
                W("tensor", eV, ptevac_prev)
            for m in range(NKT):
                tgt = (PTa if m < 8 else PTb)
                f = (lambda e, m=m, tgt=tgt:
                     e.transpose(tgt[:, (m % 8) * 128:(m % 8) * 128 + 128],
                                 VtS[m // 8][:, (m % 8) * 128:(m % 8) * 128 + 128],
                                 ident[:]))
                if m in (7, 15):
                    I("tensor", eT, f)
                else:
                    N("tensor", f)
            vtr = eT.n
            used("stg0", eT)
            used("stg1", eT)
            need("vector", f"vb{hh % 2}")
            W("vector", eT, vtr)
            I("vector", eV, lambda e, hh=hh: e.tensor_copy(Vb2[hh % 2][:, 0:1024], PTa[:]))
            I("vector", eV, lambda e, hh=hh: e.tensor_copy(Vb2[hh % 2][:, 1024:2048], PTb[:]))
            return eV.n

        kv_dma(0)
        for u in range(8):
            mtok, b, isv, tc = proj_unit_mm(0, u)
            proj_unit_evac(0, u, mtok, b, isv, tc)

        for h in range(NH):
            if h + 1 < NH:
                kv_dma(h + 1)
            ptevac_cur = v_transposes(h)
            ptevac_prev = ptevac_cur
            for j in range(NQT):
                L = 256 * (j + 1)
                ncj = (L + 511) // 512
                for kc in range(ncj):
                    nn = min(512, L - kc * 512)
                    bank = kc % 2
                    if h == 0 and j == 0 and kc < 2:
                        W("tensor", eS, q_evac_done)
                    if sb_read_tok[bank]:
                        W("tensor", eV, sb_read_tok[bank])
                    W("tensor", eS, kt_ready[h][kc])
                    I("tensor", eT, lambda e, h=h, j=j, kc=kc, nn=nn, bank=bank:
                      e.matmul(SB[:, bank * 512: bank * 512 + nn],
                               QT[:, h * 1024 + j * 128: h * 1024 + j * 128 + 128],
                               KT2[h % 2][:, kc * 512: kc * 512 + nn],
                               start=True, stop=True))
                    mm_tok = eT.n
                    W("vector", eT, mm_tok)
                    if kc == 0 and pscale_prev:
                        W("vector", eS, pscale_prev)
                    I("vector", eV, lambda e, kc=kc, nn=nn, bank=bank: e.tensor_scalar(
                        Ssb[:, kc * 512: kc * 512 + nn], SB[:, bank * 512: bank * 512 + nn],
                        0.0, -3.0e38, OP.add, OP.max, accum_out=mx4[:, kc:kc + 1]))
                    sb_read_tok[bank] = eV.n
                if h + 1 < NH:
                    u_mm = proj_unit_mm(h + 1, j)
                if prev_exp:
                    W("vector", eS, prev_exp)
                N("vector", lambda e: e.drain())
                N("vector", lambda e, ncj=ncj: e.reduce_max(
                    negm[:], mx4[:, 0:ncj], axis=AX, negate=True))
                N("vector", lambda e: e.drain())
                N("vector", lambda e, L=L: e.tensor_tensor(
                    Ssb[:, L - 256:L], Ssb[:, L - 256:L], mask_s[:], OP.add))
                I("vector", eV, lambda e: e.drain())
                mask_tok = eV.n
                W("scalar", eV, mask_tok)
                for kc in range(ncj):
                    nn = min(512, L - kc * 512)
                    I("scalar", eS, lambda e, kc=kc, nn=nn: e.activation(
                        Ssb[:, kc * 512: kc * 512 + nn], Ssb[:, kc * 512: kc * 512 + nn],
                        AF.Exp, bias=negm[:], accum_out=sumc[:, kc:kc + 1]))
                N("scalar", lambda e: e.drain())
                N("scalar", lambda e, ncj=ncj: e.activation(
                    scr4[:, 0:ncj], sumc[:, 0:ncj], AF.Identity, accum_out=sums[:]))
                I("scalar", eS, lambda e: e.drain())
                expj_tok = eS.n
                prev_exp = expj_tok
                W("vector", eS, expj_tok)
                N("vector", lambda e: e.reciprocal(rvec[:], sums[:]))
                I("vector", eV, lambda e: e.drain())
                rv_tok = eV.n
                W("scalar", eV, rv_tok)
                if trj_prev:
                    W("scalar", eT, trj_prev)
                I("scalar", eS, lambda e, L=L: e.activation(
                    Pbs[:, 0:L], Ssb[:, 0:L], AF.Identity, scale=rvec[:]))
                pscale_tok = eS.n
                pscale_prev = pscale_tok
                W("tensor", eS, pscale_tok)
                W("tensor", eV, ptevac_prev)
                nkt = 2 * (j + 1)
                for kt in range(nkt):
                    tgt = (PTa if kt < 8 else PTb)
                    f = (lambda e, kt=kt, tgt=tgt:
                         e.transpose(tgt[:, (kt % 8) * 128:(kt % 8) * 128 + 128],
                                     Pbs[:, kt * 128:(kt + 1) * 128], ident[:]))
                    if kt == min(nkt, 8) - 1 or kt == nkt - 1:
                        I("tensor", eT, f)
                    else:
                        N("tensor", f)
                trj_tok = eT.n
                trj_prev = trj_tok
                W("vector", eT, trj_tok)
                if avj_prev:
                    W("vector", eT, avj_prev)
                na = min(nkt, 8)
                I("vector", eV, lambda e, na=na: e.tensor_copy(
                    Pt[:, 0:na * 128], PTa[:, 0:na * 128]))
                if nkt > 8:
                    I("vector", eV, lambda e, nkt=nkt: e.tensor_copy(
                        Pt[:, 1024:nkt * 128], PTb[:, 0:(nkt - 8) * 128]))
                ptevac_prev = eV.n
                W("tensor", eV, ptevac_prev)
                ob = (j % 2) * 512
                need("tensor", f"opd{ob}")
                for m in range(nkt):
                    f = (lambda e, m=m, ob=ob, nkt=nkt, h=h:
                         e.matmul(OPp[:, ob: ob + 128],
                                  Vb2[h % 2][:, m * 128:(m + 1) * 128],
                                  Pt[:, m * 128:(m + 1) * 128],
                                  start=(m == 0), stop=(m == nkt - 1)))
                    if m == nkt - 1:
                        I("tensor", eT, f)
                    else:
                        N("tensor", f)
                avj_tok = eT.n
                avj_prev = avj_tok
                W("vector", eT, avj_tok)
                I("vector", eV, lambda e, h=h, j=j, ob=ob: e.tensor_copy(
                    Oattn[:, h * 1024 + j * 128: h * 1024 + j * 128 + 128],
                    OPp[:, ob: ob + 128]))
                o_evac_tok[(h, j)] = eV.n
                used(f"opd{ob}", eV)
                if h + 1 < NH:
                    proj_unit_evac(h + 1, j, *u_mm)
            used(f"kt{h % 2}", eT)
            used(f"vb{h % 2}", eT)
        last_xbkv_mm = last_xbkv_mm[0]

        attn_done_eV = eV.n
        attn_done_eT = eT.n
        tap("kt15", KT2[1][:, :], eS, eS.n)
        tap("v15", Vb2[1][:, :], eV, eV.n)
        tap("p15_7", Pbs[:, :], eS, eS.n)
        tap("oattn", Oattn[:, :], eV, attn_done_eV)

        def ln_stats_chunk(src_ap, tt, c, src_sem, src_tok2, sums_row, sq_row_ap, first, last):
            W("scalar", src_sem, src_tok2)
            need("scalar", f"gsb{c % 2}")
            need("scalar", f"mrb{c % 2}")
            I("scalar", eS, lambda e, c=c: e.activation(
                tmpb[c % 2], src_ap[:, c * 1024 + tt * 512: c * 1024 + tt * 512 + 512],
                AF.Identity))
            I("scalar", eS, lambda e, c=c: e.activation(
                tmps[c % 2], src_ap[:, c * 1024 + tt * 512: c * 1024 + tt * 512 + 512],
                AF.Square))
            ct = eS.n
            W("tensor", eS, ct)
            if first:
                need("tensor", "SBall")
            N("tensor", lambda e, c=c, sums_row=sums_row: e.matmul(
                sums_row, onescol[:], tmpb[c % 2], start=first, stop=last))
            f = (lambda e, c=c, sq_row_ap=sq_row_ap: e.matmul(
                sq_row_ap, onescol[:], tmps[c % 2], start=first, stop=last))
            I("tensor", eT, f)
            used(f"gsb{c % 2}", eT)
            used(f"mrb{c % 2}", eT)
            return eT.n

        def ln_finalize_apply(src_ap, dst32_ap, dstb_ap, tt, stat_last, sums_row, sq_row_ap,
                              src_tok, dst_wait):
            W("vector", eT, stat_last)
            I("vector", eV, lambda e, sums_row=sums_row: e.tensor_scalar_mul(m_row[:], sums_row, 1.0 / D))
            I("vector", eV, lambda e, sq_row_ap=sq_row_ap: e.tensor_scalar_mul(msq_row[:], sq_row_ap, 1.0 / D))
            t1 = eV.n
            used("SBall", eV)
            W("scalar", eV, t1)
            I("scalar", eS, lambda e: e.activation(m2_row[:], m_row[:], AF.Square))
            t2 = eS.n
            W("vector", eS, t2)
            I("vector", eV, lambda e: e.tensor_tensor(msq_row[:], msq_row[:], m2_row[:], OP.subtract))
            t3 = eV.n
            W("scalar", eV, t3)
            I("scalar", eS, lambda e: e.activation(m2_row[:], msq_row[:], AF.Sqrt, bias=c_eps[:]))
            t4 = eS.n
            W("vector", eS, t4)
            N("vector", lambda e: e.reciprocal(r_row[:], m2_row[:]))
            I("vector", eV, lambda e: e.drain())
            t5 = eV.n
            W("sync", eV, t5)
            need("sync", "mrb0")
            W("sync", dB, dB.n)
            I("sync", dB, lambda e: e.dma_start(
                mbc[:], bass.AP(m_row, 0, [[512, 1], [0, 128], [1, 512]])), 16)
            need("sync", "mrb1")
            W("sync", dB, dB.n)
            I("sync", dB, lambda e: e.dma_start(
                rbc[:], bass.AP(r_row, 0, [[512, 1], [0, 128], [1, 512]])), 16)
            bc_tok = dB.n
            W("vector", dB, bc_tok)
            for s, v in dst_wait:
                W("vector", s, v)
            mul_tok = {}
            castb_tok = {}
            for c in range(NDC):
                lo = c * 1024 + tt * 512
                s2, v2 = src_tok[c]
                W("vector", s2, v2)
                I("vector", eV, lambda e, lo=lo: e.tensor_tensor(
                    dst32_ap[:, lo:lo + 512], src_ap[:, lo:lo + 512], mbc[:], OP.subtract))
            I("vector", eV, lambda e: e.drain())
            for c in range(NDC):
                lo = c * 1024 + tt * 512
                I("vector", eV, lambda e, lo=lo: e.tensor_tensor(
                    dst32_ap[:, lo:lo + 512], dst32_ap[:, lo:lo + 512], rbc[:], OP.mult))
                mul_tok[c] = eV.n
                if dstb_ap is not None:
                    W("scalar", eV, mul_tok[c])
                    I("scalar", eS, lambda e, lo=lo: e.activation(
                        dstb_ap[:, lo:lo + 512], dst32_ap[:, lo:lo + 512], AF.Identity))
                    castb_tok[c] = eS.n
            used("mrb0", eV)
            used("mrb1", eV)
            return mul_tok, castb_tok

        W("gpsimd", eT, last_xbkv_mm)
        W("gpsimd", dG, dG.n)
        I("gpsimd", dG, lambda e: e.dma_start(
            XownA.rearrange("p (c t) -> p c t", c=16),
            xq_d.ap().rearrange("(c p) t -> p c t", p=128)), 16)
        xown_dtok = dG.n
        I("gpsimd", eG, lambda e: e.drain())
        xown_tok = eG.n

        stt_tok = {}
        W("tensor", eV, sb_read_tok[0])
        W("tensor", eV, sb_read_tok[1])
        W("tensor", eS, unit_evac[0])
        W("tensor", eS, unit_evac[1])
        for oc in range(16):
            need("gpsimd", f"wbf{oc % 4}")
            W("gpsimd", dG, dG.n)
            I("gpsimd", dG, lambda e, oc=oc: e.dma_start(
                wbf[oc % 4][:], wout_d.ap()[oc].rearrange("p c f -> p (c f)")), 16)
            I("gpsimd", eG, lambda e: e.drain())
            W("tensor", dG, dG.n)
            W("tensor", eG, eG.n)
            if oc == 0:
                W("tensor", eV, attn_done_eV)
            if oc >= 2:
                W("tensor", eV, stt_tok[(oc - 2, 1)])
            for ac in range(16):
                for th in range(2):
                    bank = (oc % 2) * 2 + th
                    f = (lambda e, oc=oc, ac=ac, th=th, bank=bank:
                         e.matmul(SB[:, bank * 512:(bank + 1) * 512],
                                  wbf[oc % 4][:, ac * 128:(ac + 1) * 128],
                                  Oattn[:, ac * 1024 + th * 512: ac * 1024 + th * 512 + 512],
                                  start=(ac == 0), stop=(ac == 15)))
                    if ac == 15 and th == 1:
                        I("tensor", eT, f)
                    else:
                        N("tensor", f)
            used(f"wbf{oc % 4}", eT)
            oproj_tok = eT.n
            W("vector", eT, oproj_tok)
            if oc == 0:
                W("vector", dG, xown_dtok)
                W("vector", eG, xown_tok)
            for th in range(2):
                bank = (oc % 2) * 2 + th
                I("vector", eV, lambda e, oc=oc, th=th, bank=bank:
                  e.scalar_tensor_tensor(
                      XownA[:, oc * 1024 + th * 512: oc * 1024 + th * 512 + 512],
                      SB[:, bank * 512:(bank + 1) * 512],
                      bout_s[:, oc:oc + 1],
                      XownA[:, oc * 1024 + th * 512: oc * 1024 + th * 512 + 512],
                      OP.add, OP.add))
                stt_tok[(oc, th)] = eV.n
        oproj_all_eT = eT.n
        need("tensor", "opd0")
        need("tensor", "opd512")
        for cs in range(16):
            st1 = ln_stats_chunk(XownA, 0, cs, eV, stt_tok[(cs, 0)],
                                 OPp[0:1, 0:512], OPp[0:1, 512:1024],
                                 first=(cs == 0), last=(cs == 15))
        ln1_statlast_0 = st1
        used("SBall", eV)

        t_resA = tap("resA", XownA[:, :], eV, eV.n)
        if t_resA is not None:
            W("scalar", dO, t_resA)
        ln1_castb = {}
        src_tok0 = {c: (eV, stt_tok[(c, 0)]) for c in range(NDC)}
        mul_tok, cb0 = ln_finalize_apply(XownA, XN1, XN1b, 0, ln1_statlast_0,
                                         OPp[0:1, 0:512], OPp[0:1, 512:1024],
                                         src_tok0, [(eT, oproj_all_eT)])
        used("opd0", eV)
        used("opd512", eV)
        ln1_castb[0] = cb0
        need("tensor", "opd0")
        need("tensor", "opd512")
        for cs in range(16):
            st2 = ln_stats_chunk(XownA, 1, cs, eV, stt_tok[(cs, 1)],
                                 OPp[0:1, 0:512], OPp[0:1, 512:1024],
                                 first=(cs == 0), last=(cs == 15))
        src_tok1 = {c: (eV, stt_tok[(c, 1)]) for c in range(NDC)}
        mul_tok, cb1 = ln_finalize_apply(XownA, XN1, XN1b, 1, st2,
                                         OPp[0:1, 0:512], OPp[0:1, 512:1024],
                                         src_tok1, [])
        used("opd0", eV)
        used("opd512", eV)
        ln1_castb[1] = cb1
        xn1b_done = eS.n
        t_xn1 = tap("xn1", XN1[:, :], eS, xn1b_done)
        if t_xn1 is not None:
            W("vector", dO, t_xn1)

        dn_evac_tok = {}
        for tt in range(NTT):
            htok = {}
            gu_state = {}

            def gu_load(ic, tt=tt):
                sg_, su_ = 2 * (ic % 2), 2 * (ic % 2) + 1
                sb4 = ic % 2
                need("sync", f"stg4{sb4}a")
                need("sync", f"stg4{sb4}b")
                W("sync", dW, dW.n)
                I("sync", dW, lambda e, ic=ic, sb4=sb4: e.dma_start(
                    stg4[sb4][:], wgu_d.ap()[ic]), 16)
                gu_dma = dW.n
                W("scalar", dW, gu_dma)
                need("scalar", f"wbf{sg_}")
                I("scalar", eS, lambda e, sg_=sg_, sb4=sb4: e.activation(
                    wbf[sg_][:], stg4[sb4][:, 0:2048], AF.Identity, bias=c128[:]))
                gcast = eS.n
                used(f"stg4{sb4}a", eS)
                W("vector", dW, gu_dma)
                need("vector", f"wbf{su_}")
                I("vector", eV, lambda e, su_=su_, sb4=sb4: e.tensor_scalar_add(
                    wbf[su_][:], stg4[sb4][:, 2048:4096], -128.0))
                ucast = eV.n
                used(f"stg4{sb4}b", eV)
                gu_state[ic] = (sg_, su_, gcast, ucast)

            def gu_compute(ic, tt=tt):
                sg_, su_, gcast, ucast = gu_state[ic]
                gb = (ic % 2) * 512
                ub = 1024 + (ic % 2) * 512
                W("tensor", eS, gcast)
                W("tensor", eV, ucast)
                if ic == 0:
                    need("tensor", "SBall")
                if ic >= 2:
                    W("tensor", eV, htok[ic - 2])
                for dc in range(NDC):
                    if ic == 0:
                        W("tensor", eS, ln1_castb[tt][dc])
                    rr = XN1b[:, dc * 1024 + tt * 512: dc * 1024 + tt * 512 + 512]
                    N("tensor", lambda e, sg_=sg_, dc=dc, gb=gb, rr=rr:
                      e.matmul(SB[:, gb:gb + 512], wbf[sg_][:, dc * 128:(dc + 1) * 128],
                               rr, start=(dc == 0), stop=(dc == NDC - 1)))
                    f = (lambda e, su_=su_, dc=dc, ub=ub, rr=rr:
                         e.matmul(SB[:, ub:ub + 512], wbf[su_][:, dc * 128:(dc + 1) * 128],
                                  rr, start=(dc == 0), stop=(dc == NDC - 1)))
                    if dc == NDC - 1:
                        I("tensor", eT, f)
                    else:
                        N("tensor", f)
                used(f"wbf{sg_}", eT)
                used(f"wbf{su_}", eT)
                gu_tok = eT.n
                W("scalar", eT, gu_tok)
                need("scalar", f"gsb{ic % 2}")
                I("scalar", eS, lambda e, ic=ic, gb=gb: e.activation(
                    gs_buf[ic % 2][:], SB[:, gb:gb + 512], AF.Silu,
                    scale=gsc_s[:, ic:ic + 1]))
                gs_tok = eS.n
                W("vector", eS, gs_tok)
                I("vector", eV, lambda e, ic=ic, ub=ub: e.scalar_tensor_tensor(
                    Hbuf[:, ic * 512:(ic + 1) * 512], SB[:, ub:ub + 512],
                    usc_s[:, ic:ic + 1], gs_buf[ic % 2][:], OP.mult, OP.mult))
                htok[ic] = eV.n
                used(f"gsb{ic % 2}", eV)
                if ic == NIC - 1:
                    used("SBall", eV)

            for ic in range(NIC + 1):
                if ic < NIC:
                    gu_load(ic)
                if ic >= 1:
                    gu_compute(ic - 1)

            if tt == 0:
                t_h0 = tap("h0", Hbuf[:, :], eV, eV.n)
                if t_h0 is not None:
                    W("tensor", dO, t_h0)
            dn_state = {}

            def dn_load(hc):
                toks = []
                for half in range(2):
                    need("sync", f"stg4{half}a")
                    need("sync", f"stg4{half}b")
                    W("sync", dW, dW.n)
                    I("sync", dW, lambda e, hc=hc, half=half: e.dma_start(
                        stg4[half][:],
                        wd_d.ap()[hc][:, half * 32:(half + 1) * 32, :].rearrange("p c f -> p (c f)")), 16)
                    h_dma = dW.n
                    for qq in range(2):
                        q = half * 2 + qq
                        eng = "scalar" if q % 2 == 0 else "vector"
                        W(eng, dW, h_dma)
                        need(eng, f"wbf{q}")
                        if eng == "scalar":
                            I("scalar", eS, lambda e, half=half, qq=qq, q=q: e.activation(
                                wbf[q][:], stg4[half][:, qq * 2048:(qq + 1) * 2048],
                                AF.Identity, bias=c128[:]))
                            toks.append((eS, eS.n))
                            used(f"stg4{half}a", eS)
                        else:
                            I("vector", eV, lambda e, half=half, qq=qq, q=q: e.tensor_scalar_add(
                                wbf[q][:], stg4[half][:, qq * 2048:(qq + 1) * 2048], -128.0))
                            toks.append((eV, eV.n))
                            used(f"stg4{half}b", eV)
                dn_state[hc] = toks

            def dn_compute(hc, tt=tt):
                wcast_tok = dn_state[hc]
                ob = (hc % 2) * 512
                need("tensor", f"opd{ob}")
                W("tensor", eV, htok[NIC - 1])
                for q in range(4):
                    s, v = wcast_tok[q]
                    W("tensor", s, v)
                for ic in range(NIC):
                    q = ic // 16
                    f = (lambda e, ic=ic, q=q, ob=ob:
                         e.matmul(OPp[:, ob: ob + 512],
                                  wbf[q % 4][:, (ic % 16) * 128:(ic % 16) * 128 + 128],
                                  Hbuf[:, ic * 512:(ic + 1) * 512],
                                  start=(ic == 0), stop=(ic == NIC - 1)))
                    if ic % 16 == 15:
                        I("tensor", eT, f)
                        used(f"wbf{q}", eT)
                    else:
                        N("tensor", f)
                dn_tok = eT.n
                W("vector", eT, dn_tok)
                lo = hc * 1024 + tt * 512
                I("vector", eV, lambda e, hc=hc, lo=lo, ob=ob: e.scalar_tensor_tensor(
                    XN1[:, lo:lo + 512], OPp[:, ob:ob + 512], dsc_s[:, hc:hc + 1],
                    XN1[:, lo:lo + 512], OP.mult, OP.add))
                dn_evac_tok[(tt, hc)] = eV.n
                used(f"opd{ob}", eV)

            for hc in range(16):
                dn_load(hc)
                dn_compute(hc)
            for hc in range(16):
                ln2_statlast = ln_stats_chunk(XN1, tt, hc, eV, dn_evac_tok[(tt, hc)],
                                              SB[0:1, 0:512], SB[0:1, 512:1024],
                                              first=(hc == 0), last=(hc == 15))

            if tt == 0:
                t_yd = tap("ydown", XN1[:, :], eV, eV.n)
                if t_yd is not None:
                    W("vector", dO, t_yd)
            src_tok = {c: (eV, dn_evac_tok[(tt, c)]) for c in range(NDC)}
            mul_tok, _ = ln_finalize_apply(XN1, XN1, None, tt, ln2_statlast,
                                           SB[0:1, 0:512], SB[0:1, 512:1024],
                                           src_tok, [])
            for c in range(NDC):
                W("gpsimd", eV, mul_tok[c])
                I("gpsimd", dO, lambda e, c=c, tt=tt: e.dma_start(
                    yt_d.ap()[c * 128:(c + 1) * 128, tt * 512:(tt + 1) * 512],
                    XN1[:, c * 1024 + tt * 512: c * 1024 + tt * 512 + 512]), 16)

        W("gpsimd", dO, dO.n)

        @block.sync
        def _(e):
            for eng, f in plan:
                if eng == "sync":
                    f(e)

        @block.gpsimd
        def _(e):
            for eng, f in plan:
                if eng == "gpsimd":
                    f(e)

        @block.scalar
        def _(e):
            for eng, f in plan:
                if eng == "scalar":
                    f(e)

        @block.vector
        def _(e):
            for eng, f in plan:
                if eng == "vector":
                    f(e)

        @block.tensor
        def _(e):
            for eng, f in plan:
                if eng == "tensor":
                    f(e)

    return nc


def kernel(L0_attn_x, L0_attn_w_qkv, L0_attn_b_qkv, L0_attn_w_out, L0_attn_b_out,
           L0_ffn_gate_w, L0_ffn_gate_scale, L0_ffn_up_w, L0_ffn_up_scale,
           L0_ffn_down_w, L0_ffn_down_scale):
    x = np.asarray(L0_attn_x, np.float32)
    wqkv = np.asarray(L0_attn_w_qkv, np.float32)
    bqkv = np.asarray(L0_attn_b_qkv, np.float32)
    wout = np.asarray(L0_attn_w_out, np.float32)
    bout = np.asarray(L0_attn_b_out, np.float32)
    gw = np.asarray(L0_ffn_gate_w)
    uw = np.asarray(L0_ffn_up_w)
    dw = np.asarray(L0_ffn_down_w)
    gsc = np.asarray(L0_ffn_gate_scale, np.float32)
    usc = np.asarray(L0_ffn_up_scale, np.float32)
    dsc = np.asarray(L0_ffn_down_scale, np.float32)

    wqkv_s = np.ascontiguousarray(wqkv.reshape(48, 128, 16, 128).transpose(0, 3, 2, 1))
    wout_s = np.ascontiguousarray(wout.reshape(16, 128, 16, 128).transpose(0, 3, 2, 1))
    wg_s = gw.astype(np.uint8).reshape(64, 128, 16, 128).transpose(0, 3, 2, 1).reshape(64, 128, 2048)
    wu_s = uw.astype(np.uint8).reshape(64, 128, 16, 128).transpose(0, 3, 2, 1).reshape(64, 128, 2048)
    wgu_s = np.ascontiguousarray(np.concatenate([wg_s, wu_s], axis=2))
    wd_s = np.ascontiguousarray(dw.astype(np.uint8).reshape(16, 128, 64, 128).transpose(0, 3, 2, 1))
    bqkv_s = np.ascontiguousarray(bqkv.reshape(48, 128).T)
    bout_s = np.ascontiguousarray(bout.reshape(16, 128).T)
    gsc_s = np.ascontiguousarray(gsc.reshape(64, 128).T)
    usc_s = np.ascontiguousarray(usc.reshape(64, 128).T)
    dsc_s = np.ascontiguousarray(dsc.reshape(16, 128).T)
    ident = np.eye(128, dtype=ml_dtypes.bfloat16)

    tri = np.triu(np.full((128, 128), -1e9, np.float32), k=1)
    in_maps = []
    for core in range(8):
        b, par = core // 2, core % 2
        xb = x[b]
        order = []
        for t in range(8):
            order += [2 * t + par, 2 * t + 1 - par]
        xkv = np.ascontiguousarray(
            np.concatenate([xb[g * 128:(g + 1) * 128] for g in order], axis=0).T)
        xq = np.ascontiguousarray(
            np.concatenate([xb[(2 * t + par) * 128:(2 * t + par + 1) * 128]
                            for t in range(8)], axis=0).T)
        mask = np.zeros((128, 256), np.float32)
        mask[:, 0:128] = tri
        if par == 0:
            mask[:, 128:256] = -1e9
        in_maps.append(dict(
            xkv=xkv, xq=xq, mask=mask, identw=ident,
            wqkv=wqkv_s, wout=wout_s, wgu=wgu_s, wd=wd_s,
            bqkv=bqkv_s, bout=bout_s, gsc=gsc_s, usc=usc_s, dsc=dsc_s))

    nc = build_nc()
    res = run_bass_kernel_spmd(nc, in_maps, core_ids=list(range(8)))

    out = np.empty((4, 2048, 2048), np.float32)
    for core in range(8):
        b, par = core // 2, core % 2
        yt = np.asarray(res.results[core]["yt"], np.float32)
        for j in range(8):
            g = 2 * j + par
            out[b, g * 128:(g + 1) * 128, :] = yt[:, j * 128:(j + 1) * 128].T
    return out


# version 22
# speedup vs baseline: 1.0778x; 1.0169x over previous
import numpy as np
import ml_dtypes
from contextlib import ExitStack

import concourse.bass as bass
import concourse.mybir as mybir
from concourse.bass_utils import run_bass_kernel_spmd

F32 = mybir.dt.float32
BF16 = mybir.dt.bfloat16
U8 = mybir.dt.uint8
AF = mybir.ActivationFunctionType
OP = mybir.AluOpType
AX = mybir.AxisListType.X

D = 2048
NDC = 16
NH = 16
HD = 128
TQ = 1024
TKV = 2048
NQT = 8
NKT = 16
II = 8192
NIC = 64
TT = 512
NTT = 2
QK_SCALE = 1.0 / float(np.sqrt(HD))


class Sem:
    def __init__(self, name):
        self.name = name
        self.h = None
        self.n = 0


def build_nc(taps=()):
    nc = bass.Bass("TRN2", target_bir_lowering=False)

    xkv_d = nc.dram_tensor("xkv", [D, TKV], F32, kind="ExternalInput")
    xq_d = nc.dram_tensor("xq", [D, TQ], F32, kind="ExternalInput")
    wqkv_d = nc.dram_tensor("wqkv", [48, 128, NDC, 128], F32, kind="ExternalInput")
    wout_d = nc.dram_tensor("wout", [16, 128, NDC, 128], F32, kind="ExternalInput")
    wgu_d = nc.dram_tensor("wgu", [NIC, 128, 4096], U8, kind="ExternalInput")
    wd_d = nc.dram_tensor("wd", [16, 128, NIC, 128], U8, kind="ExternalInput")
    bqkv_d = nc.dram_tensor("bqkv", [128, 48], F32, kind="ExternalInput")
    bout_d = nc.dram_tensor("bout", [128, 16], F32, kind="ExternalInput")
    gsc_d = nc.dram_tensor("gsc", [128, 64], F32, kind="ExternalInput")
    usc_d = nc.dram_tensor("usc", [128, 64], F32, kind="ExternalInput")
    dsc_d = nc.dram_tensor("dsc", [128, 16], F32, kind="ExternalInput")
    mask_d = nc.dram_tensor("mask", [128, 256], F32, kind="ExternalInput")
    ident_d = nc.dram_tensor("identw", [128, 128], BF16, kind="ExternalInput")
    yt_d = nc.dram_tensor("yt", [D, TQ], F32, kind="ExternalOutput")
    tap_shapes = {
        "qt": [128, 16384], "kt15": [128, 2048], "v15": [128, 2048],
        "oattn": [128, 16384], "resA": [128, 16384], "xn1": [128, 16384],
        "h0": [128, 32768], "p15_7": [128, 2048], "ydown": [128, 16384],
    }
    tap_d = {t: nc.dram_tensor("tap_" + t, [128, tap_shapes[t][1]], F32,
                               kind="ExternalOutput") for t in taps}

    plan = []

    def E(eng, f):
        plan.append((eng, f))

    sems = [Sem(s) for s in
            ("dC", "dW", "dG", "dO", "dB", "eT", "eS", "eV", "eG")]
    dC, dW, dG, dO, dB, eT, eS, eV, eG = sems

    _wait_hwm = {}

    def W(eng, sem, val):
        assert val <= sem.n, (eng, sem.name, val, sem.n)
        if val <= 0:
            return
        key = (eng, sem.name)
        if _wait_hwm.get(key, -1) >= val:
            return
        _wait_hwm[key] = val
        E(eng, lambda e, s=sem, v=val: e.wait_ge(s.h, v))

    def I(eng, sem, f, k=1):
        sem.n += k
        E(eng, lambda e, f=f, s=sem, k=k: f(e).then_inc(s.h, k))
        return sem.n

    def N(eng, f):
        E(eng, lambda e, f=f: f(e))

    with ExitStack() as es:
        ent = es.enter_context
        AXa = ent(nc.sbuf_tensor("AXa", [128, 65536], U8))
        AZa = ent(nc.sbuf_tensor("AZa", [128, 65536], U8))
        AWa = ent(nc.sbuf_tensor("AWa", [128, 32768], U8))
        wbf = [ent(nc.sbuf_tensor(f"wbf{i}", [128, 2048], BF16)) for i in range(4)]
        stg = [ent(nc.sbuf_tensor(f"stg{i}", [128, 2048], U8)) for i in range(2)]
        stg4 = [ent(nc.sbuf_tensor(f"stg4{i}", [128, 4096], U8)) for i in range(2)]
        bqkv_s = ent(nc.sbuf_tensor("bqkv_s", [128, 48], F32))
        bqs = ent(nc.sbuf_tensor("bqs", [128, 16], F32))
        bout_s = ent(nc.sbuf_tensor("bout_s", [128, 16], F32))
        gsc_s = ent(nc.sbuf_tensor("gsc_s", [128, 64], F32))
        usc_s = ent(nc.sbuf_tensor("usc_s", [128, 64], F32))
        dsc_s = ent(nc.sbuf_tensor("dsc_s", [128, 16], F32))
        mask_s = ent(nc.sbuf_tensor("mask_s", [128, 256], F32))
        ident = ent(nc.sbuf_tensor("ident", [128, 128], BF16))
        onescol = ent(nc.sbuf_tensor("onescol", [128, 1], BF16))
        c128 = ent(nc.sbuf_tensor("c128", [128, 1], F32))
        c_eps = ent(nc.sbuf_tensor("c_eps", [1, 1], F32))
        negm = ent(nc.sbuf_tensor("negm", [128, 1], F32))
        sumc = ent(nc.sbuf_tensor("sumc", [128, 4], F32))
        sums = ent(nc.sbuf_tensor("sums", [128, 1], F32))
        rvec = ent(nc.sbuf_tensor("rvec", [128, 1], F32))
        scr4 = ent(nc.sbuf_tensor("scr4", [128, 4], F32))
        mx4 = ent(nc.sbuf_tensor("mx4", [128, 4], F32))
        gs_buf = [ent(nc.sbuf_tensor(f"gsb{i}", [128, 512], F32)) for i in range(2)]
        m_row = ent(nc.sbuf_tensor("m_row", [1, 512], F32))
        msq_row = ent(nc.sbuf_tensor("msq_row", [1, 512], F32))
        m2_row = ent(nc.sbuf_tensor("m2_row", [1, 512], F32))
        r_row = ent(nc.sbuf_tensor("r_row", [1, 512], F32))
        mbc = ent(nc.sbuf_tensor("mbc", [128, 512], F32))
        rbc = ent(nc.sbuf_tensor("rbc", [128, 512], F32))
        tmpb = [gs_buf[0][:].bitcast(BF16)[:, 0:512], gs_buf[1][:].bitcast(BF16)[:, 0:512]]
        tmps = [mbc[:].bitcast(BF16)[:, 0:512], rbc[:].bitcast(BF16)[:, 0:512]]
        SB = ent(nc.psum_tensor("SB", [128, 2048], F32))
        PTa = ent(nc.psum_tensor("PTa", [128, 1024], BF16))
        PTb = ent(nc.psum_tensor("PTb", [128, 1024], BF16))
        OPp = ent(nc.psum_tensor("OPp", [128, 1024], F32))
        for s in sems:
            s.h = ent(nc.semaphore(s.name))
        block = ent(nc.Block())

        def reg(arena, off_bytes, n_elems, dt):
            esz = {F32: 4, BF16: 2, U8: 1}[dt]
            return arena[:, off_bytes: off_bytes + n_elems * esz].bitcast(dt)

        Xbkv = reg(AXa, 0, 32768, BF16)
        XownA = reg(AXa, 0, 16384, F32)
        Hbuf = reg(AXa, 0, 32768, BF16)
        QT = reg(AZa, 0, 16384, BF16)
        Oattn = reg(AZa, 32768, 16384, BF16)
        XN1 = reg(AZa, 0, 16384, F32)
        KT2 = [reg(AWa, 0, 2048, BF16), reg(AWa, 4096, 2048, BF16)]
        Vb2 = [reg(AWa, 8192, 2048, BF16), reg(AWa, 12288, 2048, BF16)]
        Ssb = reg(AWa, 16384, 2048, F32)
        Pbs = reg(AWa, 24576, 2048, BF16)
        Pt = reg(AWa, 28672, 2048, BF16)
        XN1b = reg(AWa, 0, 16384, BF16)

        buf_tok = {}

        def need(eng, key):
            if key in buf_tok:
                s, v = buf_tok[key]
                W(eng, s, v)

        def used(key, sem):
            buf_tok[key] = (sem, sem.n)

        def tap(name, src_ap, wsem, wtok):
            if name not in tap_d:
                return None
            W("gpsimd", wsem, wtok)
            I("gpsimd", dO, lambda e, name=name, src_ap=src_ap: e.dma_start(
                tap_d[name].ap(), src_ap), 16)
            return dO.n

        for dst, src in ((bqkv_s, bqkv_d), (bout_s, bout_d), (gsc_s, gsc_d),
                         (usc_s, usc_d), (dsc_s, dsc_d), (mask_s, mask_d),
                         (ident, ident_d)):
            I("sync", dC, lambda e, d=dst, s=src: e.dma_start(d[:], s.ap()), 16)
        tok_dC = dC.n

        N("vector", lambda e: e.memset(onescol[:], 1.0))
        N("vector", lambda e: e.memset(c128[:], -128.0))
        N("vector", lambda e: e.memset(c_eps[:], 1e-5))
        W("vector", dC, tok_dC)
        tok_memset = I("vector", eV, lambda e: e.tensor_scalar_mul(
            bqs[:], bqkv_s[:, 0:16], QK_SCALE))

        xkv_tok = []
        for dc in range(NDC):
            I("gpsimd", dG, lambda e, dc=dc: e.dma_start(
                Xbkv[:, dc * 2048:(dc + 1) * 2048],
                xkv_d.ap()[dc * 128:(dc + 1) * 128, :]), 16)
            xkv_tok.append(dG.n)
        I("gpsimd", eG, lambda e: e.drain())
        xcast_done = eG.n

        W("tensor", dC, tok_dC)
        W("scalar", dC, tok_dC)
        W("scalar", eV, tok_memset)
        for fc in range(NH):
            need("gpsimd", f"wbf{fc % 4}")
            W("gpsimd", dG, dG.n)
            I("gpsimd", dG, lambda e, fc=fc: e.dma_start(
                wbf[fc % 4][:],
                wqkv_d.ap()[fc].rearrange("p c f -> p (c f)")), 16)
            I("gpsimd", eG, lambda e: e.drain())
            W("tensor", dG, dG.n)
            W("tensor", eG, eG.n)
            if fc >= 2:
                W("tensor", eS, ev_tok[fc - 2])
            if fc == 0:
                W("tensor", eV, tok_memset)
            for dc in range(NDC):
                for th in range(2):
                    bank = (fc % 2) * 2 + th
                    rhs = Xbkv[:, dc * 2048:(dc + 1) * 2048].rearrange(
                        "p (a b f) -> p a b f", a=8, b=2)[:, th * 4:(th + 1) * 4, 0:1, :]
                    f = (lambda e, fc=fc, dc=dc, bank=bank, rhs=rhs:
                         e.matmul(SB[:, bank * 512:(bank + 1) * 512],
                                  wbf[fc % 4][:, dc * 128:(dc + 1) * 128], rhs,
                                  start=(dc == 0), stop=(dc == NDC - 1)))
                    if dc == NDC - 1 and th == 1:
                        I("tensor", eT, f)
                    else:
                        N("tensor", f)
            used(f"wbf{fc % 4}", eT)
            mm_tok = eT.n
            if fc == 0:
                ev_tok = {}
            W("scalar", eT, mm_tok)
            for th in range(2):
                bank = (fc % 2) * 2 + th
                I("scalar", eS, lambda e, fc=fc, th=th, bank=bank: e.activation(
                    QT[:, fc * 1024 + th * 512: fc * 1024 + th * 512 + 512],
                    SB[:, bank * 512:(bank + 1) * 512],
                    AF.Identity, bias=bqs[:, fc:fc + 1], scale=QK_SCALE))
            ev_tok[fc] = eS.n
        q_evac_done = eS.n
        tap("qt", QT[:, :], eS, q_evac_done)
        last_xbkv_mm = [None]

        VtS = [stg[0][:].bitcast(BF16), stg[1][:].bitcast(BF16)]
        o_evac_tok = {}
        sb_read_tok = [0, 0]
        unit_evac = [q_evac_done, q_evac_done]
        pscale_prev = 0
        trj_prev = 0
        avj_prev = 0
        ptevac_prev = 0
        prev_exp = 0
        kvinfo = {}
        kt_ready = {}
        vt_ready = {}

        def kv_dma(hh):
            kb, vb = (2 * hh) % 4, (2 * hh + 1) % 4
            need("gpsimd", f"wbf{kb}")
            W("gpsimd", dG, dG.n)
            I("gpsimd", dG, lambda e, hh=hh, kb=kb: e.dma_start(
                wbf[kb][:], wqkv_d.ap()[16 + hh].rearrange("p c f -> p (c f)")), 16)
            kd = dG.n
            I("gpsimd", eG, lambda e: e.drain())
            kt_ = eG.n
            need("gpsimd", f"wbf{vb}")
            W("gpsimd", dG, dG.n)
            I("gpsimd", dG, lambda e, hh=hh, vb=vb: e.dma_start(
                wbf[vb][:], wqkv_d.ap()[32 + hh].rearrange("p c f -> p (c f)")), 16)
            vd = dG.n
            I("gpsimd", eG, lambda e: e.drain())
            vt_ = eG.n
            kvinfo[hh] = (kb, vb, kd, kt_, vd, vt_)
            kt_ready[hh] = [None] * 4

        def proj_unit_mm(hh, unit):
            kb, vb, kd, kt_, vd, vt_ = kvinfo[hh]
            b = 2 + (unit % 2)
            is_v = unit < 4
            tc = unit if is_v else unit - 4
            wb_i = vb if is_v else kb
            if is_v:
                W("tensor", dG, vd)
                W("tensor", eG, vt_)
            else:
                W("tensor", dG, kd)
                W("tensor", eG, kt_)
            W("tensor", eS, unit_evac[unit % 2])
            for dc in range(NDC):
                f = (lambda e, wb_i=wb_i, dc=dc, b=b, tc=tc:
                     e.matmul(SB[:, b * 512:(b + 1) * 512],
                              wbf[wb_i][:, dc * 128:(dc + 1) * 128],
                              Xbkv[:, dc * 2048 + tc * 512: dc * 2048 + tc * 512 + 512],
                              start=(dc == 0), stop=(dc == NDC - 1)))
                if dc == NDC - 1:
                    I("tensor", eT, f)
                else:
                    N("tensor", f)
            last_xbkv_mm[0] = eT.n
            if unit == 3:
                used(f"wbf{vb}", eT)
            if unit == 7:
                used(f"wbf{kb}", eT)
            return eT.n, b, is_v, tc

        def proj_unit_evac(hh, unit, mm_tok, b, is_v, tc):
            W("scalar", eT, mm_tok)
            if is_v:
                need("scalar", f"stg{tc // 2}")
                I("scalar", eS, lambda e, hh=hh, b=b, tc=tc: e.activation(
                    VtS[tc // 2][:, (tc % 2) * 512:(tc % 2) * 512 + 512],
                    SB[:, b * 512:(b + 1) * 512],
                    AF.Identity, bias=bqkv_s[:, 32 + hh:33 + hh]))
                vt_ready[hh] = eS.n
            else:
                need("scalar", f"kt{hh % 2}")
                I("scalar", eS, lambda e, hh=hh, b=b, tc=tc: e.activation(
                    KT2[hh % 2][:, tc * 512:(tc + 1) * 512],
                    SB[:, b * 512:(b + 1) * 512],
                    AF.Identity, bias=bqkv_s[:, 16 + hh:17 + hh]))
                kt_ready[hh][tc] = eS.n
            unit_evac[unit % 2] = eS.n

        def v_transposes(hh):
            W("tensor", eS, vt_ready[hh])
            if ptevac_prev:
                W("tensor", eV, ptevac_prev)
            for m in range(NKT):
                tgt = (PTa if m < 8 else PTb)
                f = (lambda e, m=m, tgt=tgt:
                     e.transpose(tgt[:, (m % 8) * 128:(m % 8) * 128 + 128],
                                 VtS[m // 8][:, (m % 8) * 128:(m % 8) * 128 + 128],
                                 ident[:]))
                if m in (7, 15):
                    I("tensor", eT, f)
                else:
                    N("tensor", f)
            vtr = eT.n
            used("stg0", eT)
            used("stg1", eT)
            need("vector", f"vb{hh % 2}")
            W("vector", eT, vtr)
            I("vector", eV, lambda e, hh=hh: e.tensor_copy(Vb2[hh % 2][:, 0:1024], PTa[:]))
            I("vector", eV, lambda e, hh=hh: e.tensor_copy(Vb2[hh % 2][:, 1024:2048], PTb[:]))
            return eV.n

        kv_dma(0)
        for u in range(8):
            mtok, b, isv, tc = proj_unit_mm(0, u)
            proj_unit_evac(0, u, mtok, b, isv, tc)

        for h in range(NH):
            if h + 1 < NH:
                kv_dma(h + 1)
            ptevac_cur = v_transposes(h)
            ptevac_prev = ptevac_cur
            for j in range(NQT):
                L = 256 * (j + 1)
                ncj = (L + 511) // 512
                for kc in range(ncj):
                    nn = min(512, L - kc * 512)
                    bank = kc % 2
                    if h == 0 and j == 0 and kc < 2:
                        W("tensor", eS, q_evac_done)
                    if sb_read_tok[bank]:
                        W("tensor", eV, sb_read_tok[bank])
                    W("tensor", eS, kt_ready[h][kc])
                    I("tensor", eT, lambda e, h=h, j=j, kc=kc, nn=nn, bank=bank:
                      e.matmul(SB[:, bank * 512: bank * 512 + nn],
                               QT[:, h * 1024 + j * 128: h * 1024 + j * 128 + 128],
                               KT2[h % 2][:, kc * 512: kc * 512 + nn],
                               start=True, stop=True))
                    mm_tok = eT.n
                    W("vector", eT, mm_tok)
                    if kc == 0 and pscale_prev:
                        W("vector", eS, pscale_prev)
                    I("vector", eV, lambda e, kc=kc, nn=nn, bank=bank: e.tensor_scalar(
                        Ssb[:, kc * 512: kc * 512 + nn], SB[:, bank * 512: bank * 512 + nn],
                        0.0, -3.0e38, OP.add, OP.max, accum_out=mx4[:, kc:kc + 1]))
                    sb_read_tok[bank] = eV.n
                if h + 1 < NH:
                    u_mm = proj_unit_mm(h + 1, j)
                if prev_exp:
                    W("vector", eS, prev_exp)
                N("vector", lambda e: e.drain())
                N("vector", lambda e, ncj=ncj: e.reduce_max(
                    negm[:], mx4[:, 0:ncj], axis=AX, negate=True))
                N("vector", lambda e: e.drain())
                N("vector", lambda e, L=L: e.tensor_tensor(
                    Ssb[:, L - 256:L], Ssb[:, L - 256:L], mask_s[:], OP.add))
                I("vector", eV, lambda e: e.drain())
                mask_tok = eV.n
                W("scalar", eV, mask_tok)
                for kc in range(ncj):
                    nn = min(512, L - kc * 512)
                    I("scalar", eS, lambda e, kc=kc, nn=nn: e.activation(
                        Ssb[:, kc * 512: kc * 512 + nn], Ssb[:, kc * 512: kc * 512 + nn],
                        AF.Exp, bias=negm[:], accum_out=sumc[:, kc:kc + 1]))
                N("scalar", lambda e: e.drain())
                N("scalar", lambda e, ncj=ncj: e.activation(
                    scr4[:, 0:ncj], sumc[:, 0:ncj], AF.Identity, accum_out=sums[:]))
                I("scalar", eS, lambda e: e.drain())
                expj_tok = eS.n
                prev_exp = expj_tok
                W("vector", eS, expj_tok)
                N("vector", lambda e: e.reciprocal(rvec[:], sums[:]))
                I("vector", eV, lambda e: e.drain())
                rv_tok = eV.n
                W("scalar", eV, rv_tok)
                if trj_prev:
                    W("scalar", eT, trj_prev)
                I("scalar", eS, lambda e, L=L: e.activation(
                    Pbs[:, 0:L], Ssb[:, 0:L], AF.Identity, scale=rvec[:]))
                pscale_tok = eS.n
                pscale_prev = pscale_tok
                W("tensor", eS, pscale_tok)
                W("tensor", eV, ptevac_prev)
                nkt = 2 * (j + 1)
                for kt in range(nkt):
                    tgt = (PTa if kt < 8 else PTb)
                    f = (lambda e, kt=kt, tgt=tgt:
                         e.transpose(tgt[:, (kt % 8) * 128:(kt % 8) * 128 + 128],
                                     Pbs[:, kt * 128:(kt + 1) * 128], ident[:]))
                    if kt == min(nkt, 8) - 1 or kt == nkt - 1:
                        I("tensor", eT, f)
                    else:
                        N("tensor", f)
                trj_tok = eT.n
                trj_prev = trj_tok
                W("vector", eT, trj_tok)
                if avj_prev:
                    W("vector", eT, avj_prev)
                na = min(nkt, 8)
                I("vector", eV, lambda e, na=na: e.tensor_copy(
                    Pt[:, 0:na * 128], PTa[:, 0:na * 128]))
                if nkt > 8:
                    I("vector", eV, lambda e, nkt=nkt: e.tensor_copy(
                        Pt[:, 1024:nkt * 128], PTb[:, 0:(nkt - 8) * 128]))
                ptevac_prev = eV.n
                W("tensor", eV, ptevac_prev)
                ob = (j % 2) * 512
                need("tensor", f"opd{ob}")
                for m in range(nkt):
                    f = (lambda e, m=m, ob=ob, nkt=nkt, h=h:
                         e.matmul(OPp[:, ob: ob + 128],
                                  Vb2[h % 2][:, m * 128:(m + 1) * 128],
                                  Pt[:, m * 128:(m + 1) * 128],
                                  start=(m == 0), stop=(m == nkt - 1)))
                    if m == nkt - 1:
                        I("tensor", eT, f)
                    else:
                        N("tensor", f)
                avj_tok = eT.n
                avj_prev = avj_tok
                W("vector", eT, avj_tok)
                I("vector", eV, lambda e, h=h, j=j, ob=ob: e.tensor_copy(
                    Oattn[:, h * 1024 + j * 128: h * 1024 + j * 128 + 128],
                    OPp[:, ob: ob + 128]))
                o_evac_tok[(h, j)] = eV.n
                used(f"opd{ob}", eV)
                if h + 1 < NH:
                    proj_unit_evac(h + 1, j, *u_mm)
            used(f"kt{h % 2}", eT)
            used(f"vb{h % 2}", eT)
        last_xbkv_mm = last_xbkv_mm[0]

        attn_done_eV = eV.n
        attn_done_eT = eT.n
        tap("kt15", KT2[1][:, :], eS, eS.n)
        tap("v15", Vb2[1][:, :], eV, eV.n)
        tap("p15_7", Pbs[:, :], eS, eS.n)
        tap("oattn", Oattn[:, :], eV, attn_done_eV)

        def ln_stats_chunk(src_ap, tt, c, src_sem, src_tok2, sums_row, sq_row_ap, first, last):
            eng = "scalar" if c % 2 == 0 else "vector"
            W(eng, src_sem, src_tok2)
            need(eng, f"gsb{c % 2}")
            need(eng, f"mrb{c % 2}")
            if eng == "scalar":
                I("scalar", eS, lambda e, c=c: e.activation(
                    tmpb[c % 2], src_ap[:, c * 1024 + tt * 512: c * 1024 + tt * 512 + 512],
                    AF.Identity))
                I("scalar", eS, lambda e, c=c: e.activation(
                    tmps[c % 2], src_ap[:, c * 1024 + tt * 512: c * 1024 + tt * 512 + 512],
                    AF.Square))
                ct = eS.n
                W("tensor", eS, ct)
            else:
                sl = src_ap[:, c * 1024 + tt * 512: c * 1024 + tt * 512 + 512]
                I("vector", eV, lambda e, c=c, sl=sl: e.tensor_copy(tmpb[c % 2], sl))
                I("vector", eV, lambda e, c=c, sl=sl: e.tensor_tensor(
                    tmps[c % 2], sl, sl, OP.mult))
                ct = eV.n
                W("tensor", eV, ct)
            if first:
                need("tensor", "SBall")
            N("tensor", lambda e, c=c, sums_row=sums_row: e.matmul(
                sums_row, onescol[:], tmpb[c % 2], start=first, stop=last))
            f = (lambda e, c=c, sq_row_ap=sq_row_ap: e.matmul(
                sq_row_ap, onescol[:], tmps[c % 2], start=first, stop=last))
            I("tensor", eT, f)
            used(f"gsb{c % 2}", eT)
            used(f"mrb{c % 2}", eT)
            return eT.n

        def ln_finalize_apply(src_ap, dst32_ap, dstb_ap, tt, stat_last, sums_row, sq_row_ap,
                              src_tok, dst_wait):
            W("vector", eT, stat_last)
            I("vector", eV, lambda e, sums_row=sums_row: e.tensor_scalar_mul(m_row[:], sums_row, 1.0 / D))
            I("vector", eV, lambda e, sq_row_ap=sq_row_ap: e.tensor_scalar_mul(msq_row[:], sq_row_ap, 1.0 / D))
            t1 = eV.n
            used("SBall", eV)
            W("scalar", eV, t1)
            I("scalar", eS, lambda e: e.activation(m2_row[:], m_row[:], AF.Square))
            t2 = eS.n
            W("vector", eS, t2)
            I("vector", eV, lambda e: e.tensor_tensor(msq_row[:], msq_row[:], m2_row[:], OP.subtract))
            t3 = eV.n
            W("scalar", eV, t3)
            I("scalar", eS, lambda e: e.activation(m2_row[:], msq_row[:], AF.Sqrt, bias=c_eps[:]))
            t4 = eS.n
            W("vector", eS, t4)
            N("vector", lambda e: e.reciprocal(r_row[:], m2_row[:]))
            I("vector", eV, lambda e: e.drain())
            t5 = eV.n
            W("sync", eV, t5)
            need("sync", "mrb0")
            W("sync", dB, dB.n)
            I("sync", dB, lambda e: e.dma_start(
                mbc[:], bass.AP(m_row, 0, [[512, 1], [0, 128], [1, 512]])), 16)
            need("sync", "mrb1")
            W("sync", dB, dB.n)
            I("sync", dB, lambda e: e.dma_start(
                rbc[:], bass.AP(r_row, 0, [[512, 1], [0, 128], [1, 512]])), 16)
            bc_tok = dB.n
            W("vector", dB, bc_tok)
            for s, v in dst_wait:
                W("vector", s, v)
            mul_tok = {}
            castb_tok = {}
            for c in range(NDC):
                lo = c * 1024 + tt * 512
                s2, v2 = src_tok[c]
                W("vector", s2, v2)
                I("vector", eV, lambda e, lo=lo: e.tensor_tensor(
                    dst32_ap[:, lo:lo + 512], src_ap[:, lo:lo + 512], mbc[:], OP.subtract))
            I("vector", eV, lambda e: e.drain())
            for c in range(NDC):
                lo = c * 1024 + tt * 512
                I("vector", eV, lambda e, lo=lo: e.tensor_tensor(
                    dst32_ap[:, lo:lo + 512], dst32_ap[:, lo:lo + 512], rbc[:], OP.mult))
                mul_tok[c] = eV.n
                if dstb_ap is not None:
                    W("scalar", eV, mul_tok[c])
                    I("scalar", eS, lambda e, lo=lo: e.activation(
                        dstb_ap[:, lo:lo + 512], dst32_ap[:, lo:lo + 512], AF.Identity))
                    castb_tok[c] = eS.n
            used("mrb0", eV)
            used("mrb1", eV)
            return mul_tok, castb_tok

        W("gpsimd", eT, last_xbkv_mm)
        W("gpsimd", dG, dG.n)
        I("gpsimd", dG, lambda e: e.dma_start(
            XownA.rearrange("p (c t) -> p c t", c=16),
            xq_d.ap().rearrange("(c p) t -> p c t", p=128)), 16)
        xown_dtok = dG.n
        I("gpsimd", eG, lambda e: e.drain())
        xown_tok = eG.n

        stt_tok = {}
        W("tensor", eV, sb_read_tok[0])
        W("tensor", eV, sb_read_tok[1])
        W("tensor", eS, unit_evac[0])
        W("tensor", eS, unit_evac[1])
        for oc in range(16):
            need("gpsimd", f"wbf{oc % 4}")
            W("gpsimd", dG, dG.n)
            I("gpsimd", dG, lambda e, oc=oc: e.dma_start(
                wbf[oc % 4][:], wout_d.ap()[oc].rearrange("p c f -> p (c f)")), 16)
            I("gpsimd", eG, lambda e: e.drain())
            W("tensor", dG, dG.n)
            W("tensor", eG, eG.n)
            if oc == 0:
                W("tensor", eV, attn_done_eV)
            if oc >= 2:
                W("tensor", eV, stt_tok[(oc - 2, 1)])
            for ac in range(16):
                for th in range(2):
                    bank = (oc % 2) * 2 + th
                    f = (lambda e, oc=oc, ac=ac, th=th, bank=bank:
                         e.matmul(SB[:, bank * 512:(bank + 1) * 512],
                                  wbf[oc % 4][:, ac * 128:(ac + 1) * 128],
                                  Oattn[:, ac * 1024 + th * 512: ac * 1024 + th * 512 + 512],
                                  start=(ac == 0), stop=(ac == 15)))
                    if ac == 15 and th == 1:
                        I("tensor", eT, f)
                    else:
                        N("tensor", f)
            used(f"wbf{oc % 4}", eT)
            oproj_tok = eT.n
            W("vector", eT, oproj_tok)
            if oc == 0:
                W("vector", dG, xown_dtok)
                W("vector", eG, xown_tok)
            for th in range(2):
                bank = (oc % 2) * 2 + th
                I("vector", eV, lambda e, oc=oc, th=th, bank=bank:
                  e.scalar_tensor_tensor(
                      XownA[:, oc * 1024 + th * 512: oc * 1024 + th * 512 + 512],
                      SB[:, bank * 512:(bank + 1) * 512],
                      bout_s[:, oc:oc + 1],
                      XownA[:, oc * 1024 + th * 512: oc * 1024 + th * 512 + 512],
                      OP.add, OP.add))
                stt_tok[(oc, th)] = eV.n
        oproj_all_eT = eT.n
        need("tensor", "opd0")
        need("tensor", "opd512")
        for cs in range(16):
            st1 = ln_stats_chunk(XownA, 0, cs, eV, stt_tok[(cs, 0)],
                                 OPp[0:1, 0:512], OPp[0:1, 512:1024],
                                 first=(cs == 0), last=(cs == 15))
        ln1_statlast_0 = st1
        used("SBall", eV)

        t_resA = tap("resA", XownA[:, :], eV, eV.n)
        if t_resA is not None:
            W("scalar", dO, t_resA)
        ln1_castb = {}
        src_tok0 = {c: (eV, stt_tok[(c, 0)]) for c in range(NDC)}
        mul_tok, cb0 = ln_finalize_apply(XownA, XN1, XN1b, 0, ln1_statlast_0,
                                         OPp[0:1, 0:512], OPp[0:1, 512:1024],
                                         src_tok0, [(eT, oproj_all_eT)])
        used("opd0", eV)
        used("opd512", eV)
        ln1_castb[0] = cb0
        need("tensor", "opd0")
        need("tensor", "opd512")
        for cs in range(16):
            st2 = ln_stats_chunk(XownA, 1, cs, eV, stt_tok[(cs, 1)],
                                 OPp[0:1, 0:512], OPp[0:1, 512:1024],
                                 first=(cs == 0), last=(cs == 15))
        src_tok1 = {c: (eV, stt_tok[(c, 1)]) for c in range(NDC)}
        mul_tok, cb1 = ln_finalize_apply(XownA, XN1, XN1b, 1, st2,
                                         OPp[0:1, 0:512], OPp[0:1, 512:1024],
                                         src_tok1, [])
        used("opd0", eV)
        used("opd512", eV)
        ln1_castb[1] = cb1
        xn1b_done = eS.n
        t_xn1 = tap("xn1", XN1[:, :], eS, xn1b_done)
        if t_xn1 is not None:
            W("vector", dO, t_xn1)

        dn_evac_tok = {}
        for tt in range(NTT):
            htok = {}
            gu_state = {}

            def gu_load(ic, tt=tt):
                sg_, su_ = 2 * (ic % 2), 2 * (ic % 2) + 1
                sb4 = ic % 2
                need("sync", f"stg4{sb4}a")
                need("sync", f"stg4{sb4}b")
                W("sync", dW, dW.n)
                I("sync", dW, lambda e, ic=ic, sb4=sb4: e.dma_start(
                    stg4[sb4][:], wgu_d.ap()[ic]), 16)
                gu_dma = dW.n
                W("scalar", dW, gu_dma)
                need("scalar", f"wbf{sg_}")
                I("scalar", eS, lambda e, sg_=sg_, sb4=sb4: e.activation(
                    wbf[sg_][:], stg4[sb4][:, 0:2048], AF.Identity, bias=c128[:]))
                gcast = eS.n
                used(f"stg4{sb4}a", eS)
                W("vector", dW, gu_dma)
                need("vector", f"wbf{su_}")
                I("vector", eV, lambda e, su_=su_, sb4=sb4: e.tensor_scalar_add(
                    wbf[su_][:], stg4[sb4][:, 2048:4096], -128.0))
                ucast = eV.n
                used(f"stg4{sb4}b", eV)
                gu_state[ic] = (sg_, su_, gcast, ucast)

            def gu_compute(ic, tt=tt):
                sg_, su_, gcast, ucast = gu_state[ic]
                gb = (ic % 2) * 512
                ub = 1024 + (ic % 2) * 512
                W("tensor", eS, gcast)
                W("tensor", eV, ucast)
                if ic == 0:
                    need("tensor", "SBall")
                if ic >= 2:
                    W("tensor", eV, htok[ic - 2])
                for dc in range(NDC):
                    if ic == 0:
                        W("tensor", eS, ln1_castb[tt][dc])
                    rr = XN1b[:, dc * 1024 + tt * 512: dc * 1024 + tt * 512 + 512]
                    N("tensor", lambda e, sg_=sg_, dc=dc, gb=gb, rr=rr:
                      e.matmul(SB[:, gb:gb + 512], wbf[sg_][:, dc * 128:(dc + 1) * 128],
                               rr, start=(dc == 0), stop=(dc == NDC - 1)))
                    f = (lambda e, su_=su_, dc=dc, ub=ub, rr=rr:
                         e.matmul(SB[:, ub:ub + 512], wbf[su_][:, dc * 128:(dc + 1) * 128],
                                  rr, start=(dc == 0), stop=(dc == NDC - 1)))
                    if dc == NDC - 1:
                        I("tensor", eT, f)
                    else:
                        N("tensor", f)
                used(f"wbf{sg_}", eT)
                used(f"wbf{su_}", eT)
                gu_tok = eT.n
                W("scalar", eT, gu_tok)
                need("scalar", f"gsb{ic % 2}")
                I("scalar", eS, lambda e, ic=ic, gb=gb: e.activation(
                    gs_buf[ic % 2][:], SB[:, gb:gb + 512], AF.Silu,
                    scale=gsc_s[:, ic:ic + 1]))
                gs_tok = eS.n
                W("vector", eS, gs_tok)
                I("vector", eV, lambda e, ic=ic, ub=ub: e.scalar_tensor_tensor(
                    Hbuf[:, ic * 512:(ic + 1) * 512], SB[:, ub:ub + 512],
                    usc_s[:, ic:ic + 1], gs_buf[ic % 2][:], OP.mult, OP.mult))
                htok[ic] = eV.n
                used(f"gsb{ic % 2}", eV)
                if ic == NIC - 1:
                    used("SBall", eV)

            for ic in range(NIC + 1):
                if ic < NIC:
                    gu_load(ic)
                if ic >= 1:
                    gu_compute(ic - 1)

            if tt == 0:
                t_h0 = tap("h0", Hbuf[:, :], eV, eV.n)
                if t_h0 is not None:
                    W("tensor", dO, t_h0)
            dn_state = {}

            def dn_load(hc):
                toks = []
                for half in range(2):
                    need("sync", f"stg4{half}a")
                    need("sync", f"stg4{half}b")
                    W("sync", dW, dW.n)
                    I("sync", dW, lambda e, hc=hc, half=half: e.dma_start(
                        stg4[half][:],
                        wd_d.ap()[hc][:, half * 32:(half + 1) * 32, :].rearrange("p c f -> p (c f)")), 16)
                    h_dma = dW.n
                    for qq in range(2):
                        q = half * 2 + qq
                        eng = "scalar" if q % 2 == 0 else "vector"
                        W(eng, dW, h_dma)
                        need(eng, f"wbf{q}")
                        if eng == "scalar":
                            I("scalar", eS, lambda e, half=half, qq=qq, q=q: e.activation(
                                wbf[q][:], stg4[half][:, qq * 2048:(qq + 1) * 2048],
                                AF.Identity, bias=c128[:]))
                            toks.append((eS, eS.n))
                            used(f"stg4{half}a", eS)
                        else:
                            I("vector", eV, lambda e, half=half, qq=qq, q=q: e.tensor_scalar_add(
                                wbf[q][:], stg4[half][:, qq * 2048:(qq + 1) * 2048], -128.0))
                            toks.append((eV, eV.n))
                            used(f"stg4{half}b", eV)
                dn_state[hc] = toks

            def dn_compute(hc, tt=tt):
                wcast_tok = dn_state[hc]
                ob = (hc % 2) * 512
                need("tensor", f"opd{ob}")
                W("tensor", eV, htok[NIC - 1])
                for q in range(4):
                    s, v = wcast_tok[q]
                    W("tensor", s, v)
                for ic in range(NIC):
                    q = ic // 16
                    f = (lambda e, ic=ic, q=q, ob=ob:
                         e.matmul(OPp[:, ob: ob + 512],
                                  wbf[q % 4][:, (ic % 16) * 128:(ic % 16) * 128 + 128],
                                  Hbuf[:, ic * 512:(ic + 1) * 512],
                                  start=(ic == 0), stop=(ic == NIC - 1)))
                    if ic % 16 == 15:
                        I("tensor", eT, f)
                        used(f"wbf{q}", eT)
                    else:
                        N("tensor", f)
                dn_tok = eT.n
                W("vector", eT, dn_tok)
                lo = hc * 1024 + tt * 512
                I("vector", eV, lambda e, hc=hc, lo=lo, ob=ob: e.scalar_tensor_tensor(
                    XN1[:, lo:lo + 512], OPp[:, ob:ob + 512], dsc_s[:, hc:hc + 1],
                    XN1[:, lo:lo + 512], OP.mult, OP.add))
                dn_evac_tok[(tt, hc)] = eV.n
                used(f"opd{ob}", eV)

            for hc in range(16):
                dn_load(hc)
                dn_compute(hc)
            for hc in range(16):
                ln2_statlast = ln_stats_chunk(XN1, tt, hc, eV, dn_evac_tok[(tt, hc)],
                                              SB[0:1, 0:512], SB[0:1, 512:1024],
                                              first=(hc == 0), last=(hc == 15))

            if tt == 0:
                t_yd = tap("ydown", XN1[:, :], eV, eV.n)
                if t_yd is not None:
                    W("vector", dO, t_yd)
            src_tok = {c: (eV, dn_evac_tok[(tt, c)]) for c in range(NDC)}
            mul_tok, _ = ln_finalize_apply(XN1, XN1, None, tt, ln2_statlast,
                                           SB[0:1, 0:512], SB[0:1, 512:1024],
                                           src_tok, [])
            for c in range(NDC):
                W("gpsimd", eV, mul_tok[c])
                I("gpsimd", dO, lambda e, c=c, tt=tt: e.dma_start(
                    yt_d.ap()[c * 128:(c + 1) * 128, tt * 512:(tt + 1) * 512],
                    XN1[:, c * 1024 + tt * 512: c * 1024 + tt * 512 + 512]), 16)

        W("gpsimd", dO, dO.n)

        @block.sync
        def _(e):
            for eng, f in plan:
                if eng == "sync":
                    f(e)

        @block.gpsimd
        def _(e):
            for eng, f in plan:
                if eng == "gpsimd":
                    f(e)

        @block.scalar
        def _(e):
            for eng, f in plan:
                if eng == "scalar":
                    f(e)

        @block.vector
        def _(e):
            for eng, f in plan:
                if eng == "vector":
                    f(e)

        @block.tensor
        def _(e):
            for eng, f in plan:
                if eng == "tensor":
                    f(e)

    return nc


def kernel(L0_attn_x, L0_attn_w_qkv, L0_attn_b_qkv, L0_attn_w_out, L0_attn_b_out,
           L0_ffn_gate_w, L0_ffn_gate_scale, L0_ffn_up_w, L0_ffn_up_scale,
           L0_ffn_down_w, L0_ffn_down_scale):
    x = np.asarray(L0_attn_x, np.float32)
    wqkv = np.asarray(L0_attn_w_qkv, np.float32)
    bqkv = np.asarray(L0_attn_b_qkv, np.float32)
    wout = np.asarray(L0_attn_w_out, np.float32)
    bout = np.asarray(L0_attn_b_out, np.float32)
    gw = np.asarray(L0_ffn_gate_w)
    uw = np.asarray(L0_ffn_up_w)
    dw = np.asarray(L0_ffn_down_w)
    gsc = np.asarray(L0_ffn_gate_scale, np.float32)
    usc = np.asarray(L0_ffn_up_scale, np.float32)
    dsc = np.asarray(L0_ffn_down_scale, np.float32)

    wqkv_s = np.ascontiguousarray(wqkv.reshape(48, 128, 16, 128).transpose(0, 3, 2, 1))
    wout_s = np.ascontiguousarray(wout.reshape(16, 128, 16, 128).transpose(0, 3, 2, 1))
    wg_s = gw.astype(np.uint8).reshape(64, 128, 16, 128).transpose(0, 3, 2, 1).reshape(64, 128, 2048)
    wu_s = uw.astype(np.uint8).reshape(64, 128, 16, 128).transpose(0, 3, 2, 1).reshape(64, 128, 2048)
    wgu_s = np.ascontiguousarray(np.concatenate([wg_s, wu_s], axis=2))
    wd_s = np.ascontiguousarray(dw.astype(np.uint8).reshape(16, 128, 64, 128).transpose(0, 3, 2, 1))
    bqkv_s = np.ascontiguousarray(bqkv.reshape(48, 128).T)
    bout_s = np.ascontiguousarray(bout.reshape(16, 128).T)
    gsc_s = np.ascontiguousarray(gsc.reshape(64, 128).T)
    usc_s = np.ascontiguousarray(usc.reshape(64, 128).T)
    dsc_s = np.ascontiguousarray(dsc.reshape(16, 128).T)
    ident = np.eye(128, dtype=ml_dtypes.bfloat16)

    tri = np.triu(np.full((128, 128), -1e9, np.float32), k=1)
    in_maps = []
    for core in range(8):
        b, par = core // 2, core % 2
        xb = x[b]
        order = []
        for t in range(8):
            order += [2 * t + par, 2 * t + 1 - par]
        xkv = np.ascontiguousarray(
            np.concatenate([xb[g * 128:(g + 1) * 128] for g in order], axis=0).T)
        xq = np.ascontiguousarray(
            np.concatenate([xb[(2 * t + par) * 128:(2 * t + par + 1) * 128]
                            for t in range(8)], axis=0).T)
        mask = np.zeros((128, 256), np.float32)
        mask[:, 0:128] = tri
        if par == 0:
            mask[:, 128:256] = -1e9
        in_maps.append(dict(
            xkv=xkv, xq=xq, mask=mask, identw=ident,
            wqkv=wqkv_s, wout=wout_s, wgu=wgu_s, wd=wd_s,
            bqkv=bqkv_s, bout=bout_s, gsc=gsc_s, usc=usc_s, dsc=dsc_s))

    nc = build_nc()
    res = run_bass_kernel_spmd(nc, in_maps, core_ids=list(range(8)))

    out = np.empty((4, 2048, 2048), np.float32)
    for core in range(8):
        b, par = core // 2, core % 2
        yt = np.asarray(res.results[core]["yt"], np.float32)
        for j in range(8):
            g = 2 * j + par
            out[b, g * 128:(g + 1) * 128, :] = yt[:, j * 128:(j + 1) * 128].T
    return out


# version 23 (final)
# speedup vs baseline: 1.0897x; 1.0110x over previous
import numpy as np
import ml_dtypes
from contextlib import ExitStack

import concourse.bass as bass
import concourse.mybir as mybir
from concourse.bass_utils import run_bass_kernel_spmd

F32 = mybir.dt.float32
BF16 = mybir.dt.bfloat16
U8 = mybir.dt.uint8
AF = mybir.ActivationFunctionType
OP = mybir.AluOpType
AX = mybir.AxisListType.X

D = 2048
NDC = 16
NH = 16
HD = 128
TQ = 1024
TKV = 2048
NQT = 8
NKT = 16
II = 8192
NIC = 64
TT = 512
NTT = 2
QK_SCALE = 1.0 / float(np.sqrt(HD))


class Sem:
    def __init__(self, name):
        self.name = name
        self.h = None
        self.n = 0


def build_nc(taps=()):
    nc = bass.Bass("TRN2", target_bir_lowering=False)

    xkv_d = nc.dram_tensor("xkv", [D, TKV], F32, kind="ExternalInput")
    xq_d = nc.dram_tensor("xq", [D, TQ], F32, kind="ExternalInput")
    wqkv_d = nc.dram_tensor("wqkv", [48, 128, NDC, 128], F32, kind="ExternalInput")
    wout_d = nc.dram_tensor("wout", [16, 128, NDC, 128], F32, kind="ExternalInput")
    wgu_d = nc.dram_tensor("wgu", [NIC, 128, 4096], U8, kind="ExternalInput")
    wd_d = nc.dram_tensor("wd", [16, 128, NIC, 128], U8, kind="ExternalInput")
    bqkv_d = nc.dram_tensor("bqkv", [128, 48], F32, kind="ExternalInput")
    bout_d = nc.dram_tensor("bout", [128, 16], F32, kind="ExternalInput")
    gsc_d = nc.dram_tensor("gsc", [128, 64], F32, kind="ExternalInput")
    usc_d = nc.dram_tensor("usc", [128, 64], F32, kind="ExternalInput")
    dsc_d = nc.dram_tensor("dsc", [128, 16], F32, kind="ExternalInput")
    mask_d = nc.dram_tensor("mask", [128, 256], F32, kind="ExternalInput")
    ident_d = nc.dram_tensor("identw", [128, 128], BF16, kind="ExternalInput")
    yt_d = nc.dram_tensor("yt", [D, TQ], F32, kind="ExternalOutput")
    tap_shapes = {
        "qt": [128, 16384], "kt15": [128, 2048], "v15": [128, 2048],
        "oattn": [128, 16384], "resA": [128, 16384], "xn1": [128, 16384],
        "h0": [128, 32768], "p15_7": [128, 2048], "ydown": [128, 16384],
    }
    tap_d = {t: nc.dram_tensor("tap_" + t, [128, tap_shapes[t][1]], F32,
                               kind="ExternalOutput") for t in taps}

    plan = []

    def E(eng, f):
        plan.append((eng, f))

    sems = [Sem(s) for s in
            ("dC", "dW", "dG", "dO", "dB", "eT", "eS", "eV", "eG")]
    dC, dW, dG, dO, dB, eT, eS, eV, eG = sems

    _wait_hwm = {}

    def W(eng, sem, val):
        assert val <= sem.n, (eng, sem.name, val, sem.n)
        if val <= 0:
            return
        key = (eng, sem.name)
        if _wait_hwm.get(key, -1) >= val:
            return
        _wait_hwm[key] = val
        E(eng, lambda e, s=sem, v=val: e.wait_ge(s.h, v))

    def I(eng, sem, f, k=1):
        sem.n += k
        E(eng, lambda e, f=f, s=sem, k=k: f(e).then_inc(s.h, k))
        return sem.n

    def N(eng, f):
        E(eng, lambda e, f=f: f(e))

    with ExitStack() as es:
        ent = es.enter_context
        AXa = ent(nc.sbuf_tensor("AXa", [128, 65536], U8))
        AZa = ent(nc.sbuf_tensor("AZa", [128, 65536], U8))
        AWa = ent(nc.sbuf_tensor("AWa", [128, 32768], U8))
        wbf = [ent(nc.sbuf_tensor(f"wbf{i}", [128, 2048], BF16)) for i in range(4)]
        stg = [ent(nc.sbuf_tensor(f"stg{i}", [128, 2048], U8)) for i in range(2)]
        stg4 = [ent(nc.sbuf_tensor(f"stg4{i}", [128, 4096], U8)) for i in range(2)]
        bqkv_s = ent(nc.sbuf_tensor("bqkv_s", [128, 48], F32))
        bqs = ent(nc.sbuf_tensor("bqs", [128, 16], F32))
        bout_s = ent(nc.sbuf_tensor("bout_s", [128, 16], F32))
        gsc_s = ent(nc.sbuf_tensor("gsc_s", [128, 64], F32))
        usc_s = ent(nc.sbuf_tensor("usc_s", [128, 64], F32))
        dsc_s = ent(nc.sbuf_tensor("dsc_s", [128, 16], F32))
        mask_s = ent(nc.sbuf_tensor("mask_s", [128, 256], F32))
        ident = ent(nc.sbuf_tensor("ident", [128, 128], BF16))
        onescol = ent(nc.sbuf_tensor("onescol", [128, 1], BF16))
        c128 = ent(nc.sbuf_tensor("c128", [128, 1], F32))
        c_eps = ent(nc.sbuf_tensor("c_eps", [1, 1], F32))
        negm = ent(nc.sbuf_tensor("negm", [128, 1], F32))
        sumc = ent(nc.sbuf_tensor("sumc", [128, 4], F32))
        sums = ent(nc.sbuf_tensor("sums", [128, 1], F32))
        rvec = ent(nc.sbuf_tensor("rvec", [128, 1], F32))
        scr4 = ent(nc.sbuf_tensor("scr4", [128, 4], F32))
        mx4 = ent(nc.sbuf_tensor("mx4", [128, 4], F32))
        gs_buf = [ent(nc.sbuf_tensor(f"gsb{i}", [128, 512], F32)) for i in range(2)]
        m_row = ent(nc.sbuf_tensor("m_row", [1, 512], F32))
        msq_row = ent(nc.sbuf_tensor("msq_row", [1, 512], F32))
        m2_row = ent(nc.sbuf_tensor("m2_row", [1, 512], F32))
        r_row = ent(nc.sbuf_tensor("r_row", [1, 512], F32))
        mbc = ent(nc.sbuf_tensor("mbc", [128, 512], F32))
        rbc = ent(nc.sbuf_tensor("rbc", [128, 512], F32))
        tmpb = [gs_buf[0][:].bitcast(BF16)[:, 0:512], gs_buf[1][:].bitcast(BF16)[:, 0:512]]
        tmps = [mbc[:].bitcast(BF16)[:, 0:512], rbc[:].bitcast(BF16)[:, 0:512]]
        SB = ent(nc.psum_tensor("SB", [128, 2048], F32))
        PTa = ent(nc.psum_tensor("PTa", [128, 1024], BF16))
        PTb = ent(nc.psum_tensor("PTb", [128, 1024], BF16))
        OPp = ent(nc.psum_tensor("OPp", [128, 1024], F32))
        for s in sems:
            s.h = ent(nc.semaphore(s.name))
        block = ent(nc.Block())

        def reg(arena, off_bytes, n_elems, dt):
            esz = {F32: 4, BF16: 2, U8: 1}[dt]
            return arena[:, off_bytes: off_bytes + n_elems * esz].bitcast(dt)

        Xbkv = reg(AXa, 0, 32768, BF16)
        XownA = reg(AXa, 0, 16384, F32)
        Hbuf = reg(AXa, 0, 32768, BF16)
        QT = reg(AZa, 0, 16384, BF16)
        Oattn = reg(AZa, 32768, 16384, BF16)
        XN1 = reg(AZa, 0, 16384, F32)
        KT2 = [reg(AWa, 0, 2048, BF16), reg(AWa, 4096, 2048, BF16)]
        Vb2 = [reg(AWa, 8192, 2048, BF16), reg(AWa, 12288, 2048, BF16)]
        Ssb = reg(AWa, 16384, 2048, F32)
        Pbs = reg(AWa, 24576, 2048, BF16)
        Pt = reg(AWa, 28672, 2048, BF16)
        XN1b = reg(AWa, 0, 16384, BF16)

        buf_tok = {}

        def need(eng, key):
            if key in buf_tok:
                s, v = buf_tok[key]
                W(eng, s, v)

        def used(key, sem):
            buf_tok[key] = (sem, sem.n)

        def tap(name, src_ap, wsem, wtok):
            if name not in tap_d:
                return None
            W("gpsimd", wsem, wtok)
            I("gpsimd", dO, lambda e, name=name, src_ap=src_ap: e.dma_start(
                tap_d[name].ap(), src_ap), 16)
            return dO.n

        for dst, src in ((bqkv_s, bqkv_d), (bout_s, bout_d), (gsc_s, gsc_d),
                         (usc_s, usc_d), (dsc_s, dsc_d), (mask_s, mask_d),
                         (ident, ident_d)):
            I("sync", dC, lambda e, d=dst, s=src: e.dma_start(d[:], s.ap()), 16)
        tok_dC = dC.n

        N("vector", lambda e: e.memset(onescol[:], 1.0))
        N("vector", lambda e: e.memset(c128[:], -128.0))
        N("vector", lambda e: e.memset(c_eps[:], 1e-5))
        W("vector", dC, tok_dC)
        tok_memset = I("vector", eV, lambda e: e.tensor_scalar_mul(
            bqs[:], bqkv_s[:, 0:16], QK_SCALE))

        xkv_tok = []
        for dc in range(NDC):
            I("gpsimd", dG, lambda e, dc=dc: e.dma_start(
                Xbkv[:, dc * 2048:(dc + 1) * 2048],
                xkv_d.ap()[dc * 128:(dc + 1) * 128, :]), 16)
            xkv_tok.append(dG.n)
        I("gpsimd", eG, lambda e: e.drain())
        xcast_done = eG.n

        W("tensor", dC, tok_dC)
        W("scalar", dC, tok_dC)
        W("scalar", eV, tok_memset)
        for fc in range(NH):
            need("gpsimd", f"wbf{fc % 4}")
            W("gpsimd", dG, dG.n)
            I("gpsimd", dG, lambda e, fc=fc: e.dma_start(
                wbf[fc % 4][:],
                wqkv_d.ap()[fc].rearrange("p c f -> p (c f)")), 16)
            I("gpsimd", eG, lambda e: e.drain())
            W("tensor", dG, dG.n)
            W("tensor", eG, eG.n)
            if fc >= 2:
                W("tensor", eS, ev_tok[fc - 2])
            if fc == 0:
                W("tensor", eV, tok_memset)
            for dc in range(NDC):
                for th in range(2):
                    bank = (fc % 2) * 2 + th
                    rhs = Xbkv[:, dc * 2048:(dc + 1) * 2048].rearrange(
                        "p (a b f) -> p a b f", a=8, b=2)[:, th * 4:(th + 1) * 4, 0:1, :]
                    f = (lambda e, fc=fc, dc=dc, bank=bank, rhs=rhs:
                         e.matmul(SB[:, bank * 512:(bank + 1) * 512],
                                  wbf[fc % 4][:, dc * 128:(dc + 1) * 128], rhs,
                                  start=(dc == 0), stop=(dc == NDC - 1)))
                    if dc == NDC - 1 and th == 1:
                        I("tensor", eT, f)
                    else:
                        N("tensor", f)
            used(f"wbf{fc % 4}", eT)
            mm_tok = eT.n
            if fc == 0:
                ev_tok = {}
            W("scalar", eT, mm_tok)
            for th in range(2):
                bank = (fc % 2) * 2 + th
                I("scalar", eS, lambda e, fc=fc, th=th, bank=bank: e.activation(
                    QT[:, fc * 1024 + th * 512: fc * 1024 + th * 512 + 512],
                    SB[:, bank * 512:(bank + 1) * 512],
                    AF.Identity, bias=bqs[:, fc:fc + 1], scale=QK_SCALE))
            ev_tok[fc] = eS.n
        q_evac_done = eS.n
        tap("qt", QT[:, :], eS, q_evac_done)
        last_xbkv_mm = [None]

        VtS = [stg[0][:].bitcast(BF16), stg[1][:].bitcast(BF16)]
        o_evac_tok = {}
        sb_read_tok = [0, 0]
        unit_evac = [q_evac_done, q_evac_done]
        pscale_prev = 0
        trj_prev = 0
        avj_prev = 0
        ptevac_prev = 0
        prev_exp = 0
        kvinfo = {}
        kt_ready = {}
        vt_ready = {}

        def kv_dma(hh):
            kb, vb = (2 * hh) % 4, (2 * hh + 1) % 4
            need("gpsimd", f"wbf{kb}")
            W("gpsimd", dG, dG.n)
            I("gpsimd", dG, lambda e, hh=hh, kb=kb: e.dma_start(
                wbf[kb][:], wqkv_d.ap()[16 + hh].rearrange("p c f -> p (c f)")), 16)
            kd = dG.n
            I("gpsimd", eG, lambda e: e.drain())
            kt_ = eG.n
            need("gpsimd", f"wbf{vb}")
            W("gpsimd", dG, dG.n)
            I("gpsimd", dG, lambda e, hh=hh, vb=vb: e.dma_start(
                wbf[vb][:], wqkv_d.ap()[32 + hh].rearrange("p c f -> p (c f)")), 16)
            vd = dG.n
            I("gpsimd", eG, lambda e: e.drain())
            vt_ = eG.n
            kvinfo[hh] = (kb, vb, kd, kt_, vd, vt_)
            kt_ready[hh] = [None] * 4

        def proj_unit_mm(hh, unit):
            kb, vb, kd, kt_, vd, vt_ = kvinfo[hh]
            b = 2 + (unit % 2)
            is_v = unit < 4
            tc = unit if is_v else unit - 4
            wb_i = vb if is_v else kb
            if is_v:
                W("tensor", dG, vd)
                W("tensor", eG, vt_)
            else:
                W("tensor", dG, kd)
                W("tensor", eG, kt_)
            W("tensor", eS, unit_evac[unit % 2])
            for dc in range(NDC):
                f = (lambda e, wb_i=wb_i, dc=dc, b=b, tc=tc:
                     e.matmul(SB[:, b * 512:(b + 1) * 512],
                              wbf[wb_i][:, dc * 128:(dc + 1) * 128],
                              Xbkv[:, dc * 2048 + tc * 512: dc * 2048 + tc * 512 + 512],
                              start=(dc == 0), stop=(dc == NDC - 1)))
                if dc == NDC - 1:
                    I("tensor", eT, f)
                else:
                    N("tensor", f)
            last_xbkv_mm[0] = eT.n
            if unit == 3:
                used(f"wbf{vb}", eT)
            if unit == 7:
                used(f"wbf{kb}", eT)
            return eT.n, b, is_v, tc

        def proj_unit_evac(hh, unit, mm_tok, b, is_v, tc):
            W("scalar", eT, mm_tok)
            if is_v:
                need("scalar", f"stg{tc // 2}")
                I("scalar", eS, lambda e, hh=hh, b=b, tc=tc: e.activation(
                    VtS[tc // 2][:, (tc % 2) * 512:(tc % 2) * 512 + 512],
                    SB[:, b * 512:(b + 1) * 512],
                    AF.Identity, bias=bqkv_s[:, 32 + hh:33 + hh]))
                vt_ready[hh] = eS.n
            else:
                need("scalar", f"kt{hh % 2}")
                I("scalar", eS, lambda e, hh=hh, b=b, tc=tc: e.activation(
                    KT2[hh % 2][:, tc * 512:(tc + 1) * 512],
                    SB[:, b * 512:(b + 1) * 512],
                    AF.Identity, bias=bqkv_s[:, 16 + hh:17 + hh]))
                kt_ready[hh][tc] = eS.n
            unit_evac[unit % 2] = eS.n

        def v_transposes(hh):
            W("tensor", eS, vt_ready[hh])
            if ptevac_prev:
                W("tensor", eV, ptevac_prev)
            for m in range(NKT):
                tgt = (PTa if m < 8 else PTb)
                f = (lambda e, m=m, tgt=tgt:
                     e.transpose(tgt[:, (m % 8) * 128:(m % 8) * 128 + 128],
                                 VtS[m // 8][:, (m % 8) * 128:(m % 8) * 128 + 128],
                                 ident[:]))
                if m in (7, 15):
                    I("tensor", eT, f)
                else:
                    N("tensor", f)
            vtr = eT.n
            used("stg0", eT)
            used("stg1", eT)
            need("vector", f"vb{hh % 2}")
            W("vector", eT, vtr)
            I("vector", eV, lambda e, hh=hh: e.tensor_copy(Vb2[hh % 2][:, 0:1024], PTa[:]))
            I("vector", eV, lambda e, hh=hh: e.tensor_copy(Vb2[hh % 2][:, 1024:2048], PTb[:]))
            return eV.n

        kv_dma(0)
        for u in range(8):
            mtok, b, isv, tc = proj_unit_mm(0, u)
            proj_unit_evac(0, u, mtok, b, isv, tc)

        for h in range(NH):
            if h + 1 < NH:
                kv_dma(h + 1)
            ptevac_cur = v_transposes(h)
            ptevac_prev = ptevac_cur
            for j in range(NQT):
                L = 256 * (j + 1)
                ncj = (L + 511) // 512
                for kc in range(ncj):
                    nn = min(512, L - kc * 512)
                    bank = kc % 2
                    if h == 0 and j == 0 and kc < 2:
                        W("tensor", eS, q_evac_done)
                    if sb_read_tok[bank]:
                        W("tensor", eV, sb_read_tok[bank])
                    W("tensor", eS, kt_ready[h][kc])
                    I("tensor", eT, lambda e, h=h, j=j, kc=kc, nn=nn, bank=bank:
                      e.matmul(SB[:, bank * 512: bank * 512 + nn],
                               QT[:, h * 1024 + j * 128: h * 1024 + j * 128 + 128],
                               KT2[h % 2][:, kc * 512: kc * 512 + nn],
                               start=True, stop=True))
                    mm_tok = eT.n
                    W("vector", eT, mm_tok)
                    if kc == 0 and pscale_prev:
                        W("vector", eS, pscale_prev)
                    I("vector", eV, lambda e, kc=kc, nn=nn, bank=bank: e.tensor_scalar(
                        Ssb[:, kc * 512: kc * 512 + nn], SB[:, bank * 512: bank * 512 + nn],
                        0.0, -3.0e38, OP.add, OP.max, accum_out=mx4[:, kc:kc + 1]))
                    sb_read_tok[bank] = eV.n
                if h + 1 < NH:
                    u_mm = proj_unit_mm(h + 1, j)
                if prev_exp:
                    W("vector", eS, prev_exp)
                N("vector", lambda e: e.drain())
                N("vector", lambda e, ncj=ncj: e.reduce_max(
                    negm[:], mx4[:, 0:ncj], axis=AX, negate=True))
                N("vector", lambda e: e.drain())
                N("vector", lambda e, L=L: e.tensor_tensor(
                    Ssb[:, L - 256:L], Ssb[:, L - 256:L], mask_s[:], OP.add))
                I("vector", eV, lambda e: e.drain())
                mask_tok = eV.n
                W("scalar", eV, mask_tok)
                for kc in range(ncj):
                    nn = min(512, L - kc * 512)
                    I("scalar", eS, lambda e, kc=kc, nn=nn: e.activation(
                        Ssb[:, kc * 512: kc * 512 + nn], Ssb[:, kc * 512: kc * 512 + nn],
                        AF.Exp, bias=negm[:], accum_out=sumc[:, kc:kc + 1]))
                N("scalar", lambda e: e.drain())
                N("scalar", lambda e, ncj=ncj: e.activation(
                    scr4[:, 0:ncj], sumc[:, 0:ncj], AF.Identity, accum_out=sums[:]))
                I("scalar", eS, lambda e: e.drain())
                expj_tok = eS.n
                prev_exp = expj_tok
                W("vector", eS, expj_tok)
                N("vector", lambda e: e.reciprocal(rvec[:], sums[:]))
                I("vector", eV, lambda e: e.drain())
                rv_tok = eV.n
                W("scalar", eV, rv_tok)
                if trj_prev:
                    W("scalar", eT, trj_prev)
                I("scalar", eS, lambda e, L=L: e.activation(
                    Pbs[:, 0:L], Ssb[:, 0:L], AF.Identity, scale=rvec[:]))
                pscale_tok = eS.n
                pscale_prev = pscale_tok
                W("tensor", eS, pscale_tok)
                W("tensor", eV, ptevac_prev)
                nkt = 2 * (j + 1)
                for kt in range(nkt):
                    tgt = (PTa if kt < 8 else PTb)
                    f = (lambda e, kt=kt, tgt=tgt:
                         e.transpose(tgt[:, (kt % 8) * 128:(kt % 8) * 128 + 128],
                                     Pbs[:, kt * 128:(kt + 1) * 128], ident[:]))
                    if kt == min(nkt, 8) - 1 or kt == nkt - 1:
                        I("tensor", eT, f)
                    else:
                        N("tensor", f)
                trj_tok = eT.n
                trj_prev = trj_tok
                W("vector", eT, trj_tok)
                if avj_prev:
                    W("vector", eT, avj_prev)
                na = min(nkt, 8)
                I("vector", eV, lambda e, na=na: e.tensor_copy(
                    Pt[:, 0:na * 128], PTa[:, 0:na * 128]))
                if nkt > 8:
                    I("vector", eV, lambda e, nkt=nkt: e.tensor_copy(
                        Pt[:, 1024:nkt * 128], PTb[:, 0:(nkt - 8) * 128]))
                ptevac_prev = eV.n
                W("tensor", eV, ptevac_prev)
                ob = (j % 2) * 512
                need("tensor", f"opd{ob}")
                for m in range(nkt):
                    f = (lambda e, m=m, ob=ob, nkt=nkt, h=h:
                         e.matmul(OPp[:, ob: ob + 128],
                                  Vb2[h % 2][:, m * 128:(m + 1) * 128],
                                  Pt[:, m * 128:(m + 1) * 128],
                                  start=(m == 0), stop=(m == nkt - 1)))
                    if m == nkt - 1:
                        I("tensor", eT, f)
                    else:
                        N("tensor", f)
                avj_tok = eT.n
                avj_prev = avj_tok
                W("vector", eT, avj_tok)
                I("vector", eV, lambda e, h=h, j=j, ob=ob: e.tensor_copy(
                    Oattn[:, h * 1024 + j * 128: h * 1024 + j * 128 + 128],
                    OPp[:, ob: ob + 128]))
                o_evac_tok[(h, j)] = eV.n
                used(f"opd{ob}", eV)
                if h + 1 < NH:
                    proj_unit_evac(h + 1, j, *u_mm)
            used(f"kt{h % 2}", eT)
            used(f"vb{h % 2}", eT)
        last_xbkv_mm = last_xbkv_mm[0]

        attn_done_eV = eV.n
        attn_done_eT = eT.n
        tap("kt15", KT2[1][:, :], eS, eS.n)
        tap("v15", Vb2[1][:, :], eV, eV.n)
        tap("p15_7", Pbs[:, :], eS, eS.n)
        tap("oattn", Oattn[:, :], eV, attn_done_eV)

        def ln_stats_chunk(src_ap, tt, c, src_sem, src_tok2, sums_row, sq_row_ap, first, last):
            eng = "scalar" if c % 2 == 0 else "vector"
            W(eng, src_sem, src_tok2)
            need(eng, f"gsb{c % 2}")
            need(eng, f"mrb{c % 2}")
            if eng == "scalar":
                I("scalar", eS, lambda e, c=c: e.activation(
                    tmpb[c % 2], src_ap[:, c * 1024 + tt * 512: c * 1024 + tt * 512 + 512],
                    AF.Identity))
                I("scalar", eS, lambda e, c=c: e.activation(
                    tmps[c % 2], src_ap[:, c * 1024 + tt * 512: c * 1024 + tt * 512 + 512],
                    AF.Square))
                ct = eS.n
                W("tensor", eS, ct)
            else:
                sl = src_ap[:, c * 1024 + tt * 512: c * 1024 + tt * 512 + 512]
                I("vector", eV, lambda e, c=c, sl=sl: e.tensor_copy(tmpb[c % 2], sl))
                I("vector", eV, lambda e, c=c, sl=sl: e.tensor_tensor(
                    tmps[c % 2], sl, sl, OP.mult))
                ct = eV.n
                W("tensor", eV, ct)
            if first:
                need("tensor", "SBall")
            N("tensor", lambda e, c=c, sums_row=sums_row: e.matmul(
                sums_row, onescol[:], tmpb[c % 2], start=first, stop=last))
            f = (lambda e, c=c, sq_row_ap=sq_row_ap: e.matmul(
                sq_row_ap, onescol[:], tmps[c % 2], start=first, stop=last))
            I("tensor", eT, f)
            used(f"gsb{c % 2}", eT)
            used(f"mrb{c % 2}", eT)
            return eT.n

        def ln_finalize_apply(src_ap, dst32_ap, dstb_ap, tt, stat_last, sums_row, sq_row_ap,
                              src_tok, dst_wait):
            W("vector", eT, stat_last)
            I("vector", eV, lambda e, sums_row=sums_row: e.tensor_scalar_mul(m_row[:], sums_row, 1.0 / D))
            I("vector", eV, lambda e, sq_row_ap=sq_row_ap: e.tensor_scalar_mul(msq_row[:], sq_row_ap, 1.0 / D))
            t1 = eV.n
            used("SBall", eV)
            W("scalar", eV, t1)
            I("scalar", eS, lambda e: e.activation(m2_row[:], m_row[:], AF.Square))
            t2 = eS.n
            W("vector", eS, t2)
            I("vector", eV, lambda e: e.tensor_tensor(msq_row[:], msq_row[:], m2_row[:], OP.subtract))
            t3 = eV.n
            W("scalar", eV, t3)
            I("scalar", eS, lambda e: e.activation(m2_row[:], msq_row[:], AF.Sqrt, bias=c_eps[:]))
            t4 = eS.n
            W("vector", eS, t4)
            N("vector", lambda e: e.reciprocal(r_row[:], m2_row[:]))
            I("vector", eV, lambda e: e.drain())
            t5 = eV.n
            W("sync", eV, t5)
            need("sync", "mrb0")
            W("sync", dB, dB.n)
            I("sync", dB, lambda e: e.dma_start(
                mbc[:], bass.AP(m_row, 0, [[512, 1], [0, 128], [1, 512]])), 16)
            need("sync", "mrb1")
            W("sync", dB, dB.n)
            I("sync", dB, lambda e: e.dma_start(
                rbc[:], bass.AP(r_row, 0, [[512, 1], [0, 128], [1, 512]])), 16)
            bc_tok = dB.n
            W("vector", dB, bc_tok)
            for s, v in dst_wait:
                W("vector", s, v)
            mul_tok = {}
            castb_tok = {}
            for c in range(NDC):
                lo = c * 1024 + tt * 512
                s2, v2 = src_tok[c]
                W("vector", s2, v2)
                I("vector", eV, lambda e, lo=lo: e.tensor_tensor(
                    dst32_ap[:, lo:lo + 512], src_ap[:, lo:lo + 512], mbc[:], OP.subtract))
            I("vector", eV, lambda e: e.drain())
            for c in range(NDC):
                lo = c * 1024 + tt * 512
                I("vector", eV, lambda e, lo=lo: e.tensor_tensor(
                    dst32_ap[:, lo:lo + 512], dst32_ap[:, lo:lo + 512], rbc[:], OP.mult))
                mul_tok[c] = eV.n
                if dstb_ap is not None:
                    W("scalar", eV, mul_tok[c])
                    I("scalar", eS, lambda e, lo=lo: e.activation(
                        dstb_ap[:, lo:lo + 512], dst32_ap[:, lo:lo + 512], AF.Identity))
                    castb_tok[c] = eS.n
            used("mrb0", eV)
            used("mrb1", eV)
            return mul_tok, castb_tok

        W("gpsimd", eT, last_xbkv_mm)
        W("gpsimd", dG, dG.n)
        I("gpsimd", dG, lambda e: e.dma_start(
            XownA.rearrange("p (c t) -> p c t", c=16),
            xq_d.ap().rearrange("(c p) t -> p c t", p=128)), 16)
        xown_dtok = dG.n
        I("gpsimd", eG, lambda e: e.drain())
        xown_tok = eG.n

        stt_tok = {}
        W("tensor", eV, sb_read_tok[0])
        W("tensor", eV, sb_read_tok[1])
        W("tensor", eS, unit_evac[0])
        W("tensor", eS, unit_evac[1])
        for oc in range(16):
            need("gpsimd", f"wbf{oc % 4}")
            W("gpsimd", dG, dG.n)
            I("gpsimd", dG, lambda e, oc=oc: e.dma_start(
                wbf[oc % 4][:], wout_d.ap()[oc].rearrange("p c f -> p (c f)")), 16)
            I("gpsimd", eG, lambda e: e.drain())
            W("tensor", dG, dG.n)
            W("tensor", eG, eG.n)
            if oc == 0:
                W("tensor", eV, attn_done_eV)
            if oc >= 2:
                W("tensor", eV, stt_tok[(oc - 2, 1)])
            for ac in range(16):
                for th in range(2):
                    bank = (oc % 2) * 2 + th
                    f = (lambda e, oc=oc, ac=ac, th=th, bank=bank:
                         e.matmul(SB[:, bank * 512:(bank + 1) * 512],
                                  wbf[oc % 4][:, ac * 128:(ac + 1) * 128],
                                  Oattn[:, ac * 1024 + th * 512: ac * 1024 + th * 512 + 512],
                                  start=(ac == 0), stop=(ac == 15)))
                    if ac == 15 and th == 1:
                        I("tensor", eT, f)
                    else:
                        N("tensor", f)
            used(f"wbf{oc % 4}", eT)
            oproj_tok = eT.n
            W("vector", eT, oproj_tok)
            if oc == 0:
                W("vector", dG, xown_dtok)
                W("vector", eG, xown_tok)
            for th in range(2):
                bank = (oc % 2) * 2 + th
                I("vector", eV, lambda e, oc=oc, th=th, bank=bank:
                  e.scalar_tensor_tensor(
                      XownA[:, oc * 1024 + th * 512: oc * 1024 + th * 512 + 512],
                      SB[:, bank * 512:(bank + 1) * 512],
                      bout_s[:, oc:oc + 1],
                      XownA[:, oc * 1024 + th * 512: oc * 1024 + th * 512 + 512],
                      OP.add, OP.add))
                stt_tok[(oc, th)] = eV.n
        oproj_all_eT = eT.n
        need("tensor", "opd0")
        need("tensor", "opd512")
        for cs in range(16):
            st1 = ln_stats_chunk(XownA, 0, cs, eV, stt_tok[(cs, 0)],
                                 OPp[0:1, 0:512], OPp[0:1, 512:1024],
                                 first=(cs == 0), last=(cs == 15))
        ln1_statlast_0 = st1
        used("SBall", eV)

        t_resA = tap("resA", XownA[:, :], eV, eV.n)
        if t_resA is not None:
            W("scalar", dO, t_resA)
        ln1_castb = {}
        src_tok0 = {c: (eV, stt_tok[(c, 0)]) for c in range(NDC)}
        mul_tok, cb0 = ln_finalize_apply(XownA, XN1, XN1b, 0, ln1_statlast_0,
                                         OPp[0:1, 0:512], OPp[0:1, 512:1024],
                                         src_tok0, [(eT, oproj_all_eT)])
        used("opd0", eV)
        used("opd512", eV)
        ln1_castb[0] = cb0
        need("tensor", "opd0")
        need("tensor", "opd512")
        for cs in range(16):
            st2 = ln_stats_chunk(XownA, 1, cs, eV, stt_tok[(cs, 1)],
                                 OPp[0:1, 0:512], OPp[0:1, 512:1024],
                                 first=(cs == 0), last=(cs == 15))
        src_tok1 = {c: (eV, stt_tok[(c, 1)]) for c in range(NDC)}
        mul_tok, cb1 = ln_finalize_apply(XownA, XN1, XN1b, 1, st2,
                                         OPp[0:1, 0:512], OPp[0:1, 512:1024],
                                         src_tok1, [])
        used("opd0", eV)
        used("opd512", eV)
        ln1_castb[1] = cb1
        xn1b_done = eS.n
        t_xn1 = tap("xn1", XN1[:, :], eS, xn1b_done)
        if t_xn1 is not None:
            W("vector", dO, t_xn1)

        dn_evac_tok = {}
        for tt in range(NTT):
            htok = {}
            gu_state = {}

            def gu_load(ic, tt=tt):
                sg_, su_ = 2 * (ic % 2), 2 * (ic % 2) + 1
                sb4 = ic % 2
                need("sync", f"stg4{sb4}a")
                need("sync", f"stg4{sb4}b")
                W("sync", dW, dW.n)
                I("sync", dW, lambda e, ic=ic, sb4=sb4: e.dma_start(
                    stg4[sb4][:], wgu_d.ap()[ic]), 16)
                gu_dma = dW.n
                W("scalar", dW, gu_dma)
                need("scalar", f"wbf{sg_}")
                I("scalar", eS, lambda e, sg_=sg_, sb4=sb4: e.activation(
                    wbf[sg_][:], stg4[sb4][:, 0:2048], AF.Identity, bias=c128[:]))
                gcast = eS.n
                used(f"stg4{sb4}a", eS)
                W("vector", dW, gu_dma)
                need("vector", f"wbf{su_}")
                I("vector", eV, lambda e, su_=su_, sb4=sb4: e.tensor_scalar_add(
                    wbf[su_][:], stg4[sb4][:, 2048:4096], -128.0))
                ucast = eV.n
                used(f"stg4{sb4}b", eV)
                gu_state[ic] = (sg_, su_, gcast, ucast)

            def gu_compute(ic, tt=tt):
                sg_, su_, gcast, ucast = gu_state[ic]
                gb = (ic % 2) * 512
                ub = 1024 + (ic % 2) * 512
                W("tensor", eS, gcast)
                W("tensor", eV, ucast)
                if ic == 0:
                    need("tensor", "SBall")
                if ic >= 2:
                    W("tensor", eV, htok[ic - 2])
                for dc in range(NDC):
                    if ic == 0:
                        W("tensor", eS, ln1_castb[tt][dc])
                    rr = XN1b[:, dc * 1024 + tt * 512: dc * 1024 + tt * 512 + 512]
                    N("tensor", lambda e, sg_=sg_, dc=dc, gb=gb, rr=rr:
                      e.matmul(SB[:, gb:gb + 512], wbf[sg_][:, dc * 128:(dc + 1) * 128],
                               rr, start=(dc == 0), stop=(dc == NDC - 1)))
                    f = (lambda e, su_=su_, dc=dc, ub=ub, rr=rr:
                         e.matmul(SB[:, ub:ub + 512], wbf[su_][:, dc * 128:(dc + 1) * 128],
                                  rr, start=(dc == 0), stop=(dc == NDC - 1)))
                    if dc == NDC - 1:
                        I("tensor", eT, f)
                    else:
                        N("tensor", f)
                used(f"wbf{sg_}", eT)
                used(f"wbf{su_}", eT)
                gu_tok = eT.n
                W("scalar", eT, gu_tok)
                need("scalar", f"gsb{ic % 2}")
                I("scalar", eS, lambda e, ic=ic, gb=gb: e.activation(
                    gs_buf[ic % 2][:], SB[:, gb:gb + 512], AF.Silu,
                    scale=gsc_s[:, ic:ic + 1]))
                gs_tok = eS.n
                W("vector", eS, gs_tok)
                I("vector", eV, lambda e, ic=ic, ub=ub: e.scalar_tensor_tensor(
                    Hbuf[:, ic * 512:(ic + 1) * 512], SB[:, ub:ub + 512],
                    usc_s[:, ic:ic + 1], gs_buf[ic % 2][:], OP.mult, OP.mult))
                htok[ic] = eV.n
                used(f"gsb{ic % 2}", eV)
                if ic == NIC - 1:
                    used("SBall", eV)

            for ic in range(NIC + 1):
                if ic < NIC:
                    gu_load(ic)
                if ic >= 1:
                    gu_compute(ic - 1)

            if tt == 0:
                t_h0 = tap("h0", Hbuf[:, :], eV, eV.n)
                if t_h0 is not None:
                    W("tensor", dO, t_h0)
            dn_state = {}

            def dn_load(hc):
                toks = []
                for half in range(2):
                    need("sync", f"stg4{half}a")
                    need("sync", f"stg4{half}b")
                    W("sync", dW, dW.n)
                    I("sync", dW, lambda e, hc=hc, half=half: e.dma_start(
                        stg4[half][:],
                        wd_d.ap()[hc][:, half * 32:(half + 1) * 32, :].rearrange("p c f -> p (c f)")), 16)
                    h_dma = dW.n
                    for qq in range(2):
                        q = half * 2 + qq
                        W("scalar", dW, h_dma)
                        need("scalar", f"wbf{q}")
                        I("scalar", eS, lambda e, half=half, qq=qq, q=q: e.activation(
                            wbf[q][:], stg4[half][:, qq * 2048:(qq + 1) * 2048],
                            AF.Identity, bias=c128[:]))
                        toks.append((eS, eS.n))
                        used(f"stg4{half}{'a' if qq == 0 else 'b'}", eS)
                dn_state[hc] = toks

            def dn_compute(hc, tt=tt):
                wcast_tok = dn_state[hc]
                ob = (hc % 2) * 512
                need("tensor", f"opd{ob}")
                W("tensor", eV, htok[NIC - 1])
                for q in range(4):
                    s, v = wcast_tok[q]
                    W("tensor", s, v)
                for ic in range(NIC):
                    q = ic // 16
                    f = (lambda e, ic=ic, q=q, ob=ob:
                         e.matmul(OPp[:, ob: ob + 512],
                                  wbf[q % 4][:, (ic % 16) * 128:(ic % 16) * 128 + 128],
                                  Hbuf[:, ic * 512:(ic + 1) * 512],
                                  start=(ic == 0), stop=(ic == NIC - 1)))
                    if ic % 16 == 15:
                        I("tensor", eT, f)
                        used(f"wbf{q}", eT)
                    else:
                        N("tensor", f)
                dn_tok = eT.n
                W("vector", eT, dn_tok)
                lo = hc * 1024 + tt * 512
                I("vector", eV, lambda e, hc=hc, lo=lo, ob=ob: e.scalar_tensor_tensor(
                    XN1[:, lo:lo + 512], OPp[:, ob:ob + 512], dsc_s[:, hc:hc + 1],
                    XN1[:, lo:lo + 512], OP.mult, OP.add))
                dn_evac_tok[(tt, hc)] = eV.n
                used(f"opd{ob}", eV)

            for hc in range(16):
                dn_load(hc)
                dn_compute(hc)
            for hc in range(16):
                ln2_statlast = ln_stats_chunk(XN1, tt, hc, eV, dn_evac_tok[(tt, hc)],
                                              SB[0:1, 0:512], SB[0:1, 512:1024],
                                              first=(hc == 0), last=(hc == 15))

            if tt == 0:
                t_yd = tap("ydown", XN1[:, :], eV, eV.n)
                if t_yd is not None:
                    W("vector", dO, t_yd)
            src_tok = {c: (eV, dn_evac_tok[(tt, c)]) for c in range(NDC)}
            mul_tok, _ = ln_finalize_apply(XN1, XN1, None, tt, ln2_statlast,
                                           SB[0:1, 0:512], SB[0:1, 512:1024],
                                           src_tok, [])
            for c in range(NDC):
                W("gpsimd", eV, mul_tok[c])
                I("gpsimd", dO, lambda e, c=c, tt=tt: e.dma_start(
                    yt_d.ap()[c * 128:(c + 1) * 128, tt * 512:(tt + 1) * 512],
                    XN1[:, c * 1024 + tt * 512: c * 1024 + tt * 512 + 512]), 16)

        W("gpsimd", dO, dO.n)

        @block.sync
        def _(e):
            for eng, f in plan:
                if eng == "sync":
                    f(e)

        @block.gpsimd
        def _(e):
            for eng, f in plan:
                if eng == "gpsimd":
                    f(e)

        @block.scalar
        def _(e):
            for eng, f in plan:
                if eng == "scalar":
                    f(e)

        @block.vector
        def _(e):
            for eng, f in plan:
                if eng == "vector":
                    f(e)

        @block.tensor
        def _(e):
            for eng, f in plan:
                if eng == "tensor":
                    f(e)

    return nc


def kernel(L0_attn_x, L0_attn_w_qkv, L0_attn_b_qkv, L0_attn_w_out, L0_attn_b_out,
           L0_ffn_gate_w, L0_ffn_gate_scale, L0_ffn_up_w, L0_ffn_up_scale,
           L0_ffn_down_w, L0_ffn_down_scale):
    x = np.asarray(L0_attn_x, np.float32)
    wqkv = np.asarray(L0_attn_w_qkv, np.float32)
    bqkv = np.asarray(L0_attn_b_qkv, np.float32)
    wout = np.asarray(L0_attn_w_out, np.float32)
    bout = np.asarray(L0_attn_b_out, np.float32)
    gw = np.asarray(L0_ffn_gate_w)
    uw = np.asarray(L0_ffn_up_w)
    dw = np.asarray(L0_ffn_down_w)
    gsc = np.asarray(L0_ffn_gate_scale, np.float32)
    usc = np.asarray(L0_ffn_up_scale, np.float32)
    dsc = np.asarray(L0_ffn_down_scale, np.float32)

    wqkv_s = np.ascontiguousarray(wqkv.reshape(48, 128, 16, 128).transpose(0, 3, 2, 1))
    wout_s = np.ascontiguousarray(wout.reshape(16, 128, 16, 128).transpose(0, 3, 2, 1))
    wg_s = gw.astype(np.uint8).reshape(64, 128, 16, 128).transpose(0, 3, 2, 1).reshape(64, 128, 2048)
    wu_s = uw.astype(np.uint8).reshape(64, 128, 16, 128).transpose(0, 3, 2, 1).reshape(64, 128, 2048)
    wgu_s = np.ascontiguousarray(np.concatenate([wg_s, wu_s], axis=2))
    wd_s = np.ascontiguousarray(dw.astype(np.uint8).reshape(16, 128, 64, 128).transpose(0, 3, 2, 1))
    bqkv_s = np.ascontiguousarray(bqkv.reshape(48, 128).T)
    bout_s = np.ascontiguousarray(bout.reshape(16, 128).T)
    gsc_s = np.ascontiguousarray(gsc.reshape(64, 128).T)
    usc_s = np.ascontiguousarray(usc.reshape(64, 128).T)
    dsc_s = np.ascontiguousarray(dsc.reshape(16, 128).T)
    ident = np.eye(128, dtype=ml_dtypes.bfloat16)

    tri = np.triu(np.full((128, 128), -1e9, np.float32), k=1)
    in_maps = []
    for core in range(8):
        b, par = core // 2, core % 2
        xb = x[b]
        order = []
        for t in range(8):
            order += [2 * t + par, 2 * t + 1 - par]
        xkv = np.ascontiguousarray(
            np.concatenate([xb[g * 128:(g + 1) * 128] for g in order], axis=0).T)
        xq = np.ascontiguousarray(
            np.concatenate([xb[(2 * t + par) * 128:(2 * t + par + 1) * 128]
                            for t in range(8)], axis=0).T)
        mask = np.zeros((128, 256), np.float32)
        mask[:, 0:128] = tri
        if par == 0:
            mask[:, 128:256] = -1e9
        in_maps.append(dict(
            xkv=xkv, xq=xq, mask=mask, identw=ident,
            wqkv=wqkv_s, wout=wout_s, wgu=wgu_s, wd=wd_s,
            bqkv=bqkv_s, bout=bout_s, gsc=gsc_s, usc=usc_s, dsc=dsc_s))

    nc = build_nc()
    res = run_bass_kernel_spmd(nc, in_maps, core_ids=list(range(8)))

    out = np.empty((4, 2048, 2048), np.float32)
    for core in range(8):
        b, par = core // 2, core % 2
        yt = np.asarray(res.results[core]["yt"], np.float32)
        for j in range(8):
            g = 2 * j + par
            out[b, g * 128:(g + 1) * 128, :] = yt[:, j * 128:(j + 1) * 128].T
    return out
